# version 04
# baseline (speedup 1.0000x reference)
import numpy as np
import ml_dtypes
from contextlib import ExitStack

import concourse.bass as bass
import concourse.mybir as mybir
from concourse.bass_utils import run_bass_kernel_spmd

F32 = mybir.dt.float32
BF16 = mybir.dt.bfloat16
U8 = mybir.dt.uint8
AF = mybir.ActivationFunctionType
ALU = mybir.AluOpType
bf16 = ml_dtypes.bfloat16

DM = 2048
NH = 32
TOWN = 1024
Q = 128
NCHUNK = TOWN // Q
NKC = DM // 128
BIG = 3.0e38
EPS = 1e-5

ENGS = ("tensor", "vector", "scalar", "gpsimd", "sync")
DEBUG = False
TRUNCATE = None
SSD_MODE = "full"
_LAST_RES = None


class Stream:
    __slots__ = ("h", "n", "step", "name")

    def __init__(self, h, name, step):
        self.h = h
        self.n = 0
        self.step = step
        self.name = name


class Builder:
    def __init__(self, nc, stack):
        self.nc = nc
        self.ops = {e: [] for e in ENGS}
        self.streams = []
        self.stack = stack

    def stream(self, name, step=1):
        h = self.stack.enter_context(self.nc.semaphore(name))
        s = Stream(h, name, step)
        self.streams.append(s)
        return s

    def emit(self, eng, fn, deps=(), inc=None):
        tok = None
        if inc is not None:
            inc.n += inc.step
            tok = (inc, inc.n)
        dd = tuple(t for t in deps if t is not None)
        self.ops[eng].append((fn, dd, inc, inc.step if inc else 0))
        return tok

    def frontier(self):
        return [(s, s.n) for s in self.streams if s.n > 0]

    def flush(self, block):
        for e in ENGS:
            ops = self.ops[e]

            def make(ops):
                def f(eng):
                    lastw = {}
                    for fn, deps, st, stp in ops:
                        for (ds, dv) in deps:
                            if dv > 0 and lastw.get(ds.name, 0) < dv:
                                eng.wait_ge(ds.h, dv)
                                lastw[ds.name] = dv
                        ins = fn(eng)
                        if st is not None:
                            ins.then_inc(st.h, stp)
                return f

            getattr(block, e)(make(ops))


class Bank:
    def __init__(self):
        self.wtok = None
        self.rtoks = []

    def wdeps(self):
        d = list(self.rtoks)
        if self.wtok is not None:
            d.append(self.wtok)
        self.rtoks = []
        return d

    def wrote(self, tok):
        self.wtok = tok

    def rdeps(self):
        return [self.wtok] if self.wtok is not None else []

    def read(self, tok):
        if tok is not None:
            self.rtoks.append(tok)


def build_nc():
    nc = bass.Bass("TRN2", target_bir_lowering=False)

    def dt_in(name, shape, dt=F32):
        return nc.dram_tensor(name, list(shape), dt, kind="ExternalInput").ap()

    u_own = dt_in("u_own", (DM, TOWN))
    u_pre = dt_in("u_pre", (DM, TOWN))
    w_inT = dt_in("w_inT", (DM, 8224))
    w_outT = dt_in("w_outT", (2048, DM))
    wgT = dt_in("wgT", (DM, 8192))
    wuT = dt_in("wuT", (DM, 8192))
    wdT = dt_in("wdT", (8192, DM))
    convw_in = dt_in("convw", (128, 192))
    convb_in = dt_in("convb", (128, 48))
    zb_in = dt_in("zb", (128, 16))
    drow_in = dt_in("drow", (1, NH))
    gate_in = dt_in("gate", (128, 1))
    identf_in = dt_in("identf", (128, 128))
    identbf_in = dt_in("identbf", (128, 128), BF16)
    triu_in = dt_in("triu", (128, 128))
    negmask_in = dt_in("negmaskT", (128, 128), BF16)
    onesrow_in = dt_in("onesrow", (1, 128))
    out_feat = nc.dram_tensor("out_feat", [DM, TOWN], F32, kind="ExternalOutput").ap()

    stack = ExitStack()
    with stack:
        LIMIT = 212608
        arena = stack.enter_context(nc.sbuf_tensor("arena", [128, LIMIT], U8))

        def carve(off, rows, cols, dt, p0=0):
            nb = cols * mybir.dt.size(dt)
            assert off % 4 == 0 and off + nb <= LIMIT, (off, nb)
            return arena[p0:p0 + rows, off:off + nb].bitcast(dt)

        identf = carve(0, 128, 128, F32)
        triu = carve(512, 128, 128, F32)
        identbf = carve(1024, 128, 128, BF16)
        negmask = carve(1280, 128, 128, BF16)
        convw = carve(1536, 128, 192, F32)
        convb = carve(2304, 128, 48, F32)
        zbias = carve(2496, 128, 16, F32)
        d_bc = carve(2560, 128, 32, F32)
        gate_c = carve(2688, 128, 1, F32)
        onescol_bf = carve(2692, 128, 1, BF16)
        eps_c = carve(2696, 1, 1, F32)
        onesrow = carve(2752, 1, 128, F32)
        drow = carve(3264, 1, 32, F32)
        S_f = [carve(3392, 128, 1024, F32), carve(7488, 128, 1024, F32)]
        S_bf = [carve(11584, 128, 1024, BF16), carve(13632, 128, 1024, BF16)]
        x1_o15 = carve(3392, 128, 1024, F32)
        woutsl = [carve(7488, 128, 2048, BF16), carve(11584, 128, 2048, BF16)]
        silu4 = [carve(7488 + 1024 * i, 128, 512, BF16) for i in range(4)]
        XBC0 = 15680
        xbc = carve(XBC0, 128, 48 * 1027, BF16)

        def xbc_tile(t, c0, c1):
            return xbc[:, t * 1027 + c0: t * 1027 + c1]

        gated = carve(XBC0, 128, 64 * 512, BF16)
        x1_bf = carve(XBC0 + 65536, 128, 16 * 1024, BF16)
        lnxb = [carve(XBC0, 128, 1024, BF16), carve(XBC0 + 2048, 128, 1024, BF16)]
        lnsq = [carve(XBC0 + 4096, 128, 1024, BF16), carve(XBC0 + 6144, 128, 1024, BF16)]
        ln_m = carve(XBC0 + 8192, 1, 512, F32)
        ln_msq = carve(XBC0 + 10240, 1, 512, F32)
        ln_var = carve(XBC0 + 12288, 1, 512, F32)
        ln_rstd = carve(XBC0 + 14336, 1, 512, F32)
        ln_m2 = carve(XBC0 + 16384, 1, 512, F32)
        mb_sb = carve(XBC0 + 18432, 128, 512, F32)
        ures = [carve(XBC0 + 20480, 128, 1024, F32), carve(XBC0 + 24576, 128, 1024, F32)]
        UBF0 = 114272
        u_bf = carve(UBF0, 128, 16 * 1024, BF16)
        yg = carve(UBF0, 128, 16 * 1024, BF16)
        wgsl = [carve(UBF0, 128, 2048, BF16), carve(UBF0 + 4096, 128, 2048, BF16)]
        wusl = [carve(UBF0 + 8192, 128, 2048, BF16), carve(UBF0 + 12288, 128, 2048, BF16)]
        wdsl = [carve(UBF0 + 16384, 128, 4096, BF16), carve(UBF0 + 24576, 128, 4096, BF16)]
        SIL0 = 147040
        silu_z = carve(SIL0, 128, 16 * 1024, BF16)
        x1_lo = carve(SIL0, 128, 8 * 1024, F32)
        TOP0 = 179808
        winsl = [carve(TOP0, 128, 2048, BF16), carve(TOP0 + 4096, 128, 2048, BF16)]
        SS0 = TOP0 + 8192
        cacc = [carve(SS0, 128, 512, F32), carve(SS0 + 2048, 128, 512, F32)]
        a_tm_sb = carve(SS0 + 4096, 128, 32, F32)
        negp_tm = carve(SS0 + 4224, 128, 32, F32)
        negp_rows2 = carve(SS0 + 4352, 32, 256, F32)
        wstate = carve(SS0 + 5376, 128, 32, F32)
        lamb = carve(SS0 + 5504, 128, 32, F32)
        wslog = carve(SS0 + 5632, 128, 32, F32)
        L_sb = [carve(SS0 + 5760, 128, 128, F32), carve(SS0 + 6272, 128, 128, F32)]
        M_sb4 = [carve(SS0 + 6784 + 256 * i, 128, 128, BF16) for i in range(4)]
        wb_sb = [carve(SS0 + 7808, 128, 128, F32), carve(SS0 + 8320, 128, 128, F32)]
        ct_sb4 = [carve(SS0 + 8832 + 256 * i, 128, 128, BF16) for i in range(4)]
        xt_sb = [carve(SS0 + 9856, 128, 128, BF16), carve(SS0 + 10112, 128, 128, BF16)]
        bt_sb = [carve(SS0 + 10368, 128, 64, BF16), carve(SS0 + 10496, 128, 64, BF16)]
        yr_sb = [carve(SS0 + 10624, 128, 128, F32), carve(SS0 + 11136, 128, 128, F32)]
        halo_sb = carve(SS0 + 11648, 128, 144, BF16)
        A_fm = carve(SS0 + 11936, 32, 1024, F32)
        sp_t2 = carve(SS0 + 16032, 32, 1024, F32)
        a_fm = carve(SS0 + 20128, 32, 1024, F32)
        assert SS0 + 24224 <= LIMIT
        x1_mid = carve(TOP0, 128, 7 * 1024, F32)

        def x1v(o):
            if o < 8:
                return x1_lo[:, o * 1024:(o + 1) * 1024]
            if o < 15:
                return x1_mid[:, (o - 8) * 1024:(o - 7) * 1024]
            return x1_o15

        pairA = stack.enter_context(nc.psum_tensor("pairA", [128, 1024], F32))
        pairB = stack.enter_context(nc.psum_tensor("pairB", [128, 1024], F32))
        ssdP = [stack.enter_context(nc.psum_tensor(f"ssdP{i}", [128, 512], F32)) for i in range(2)]
        ssdT = [stack.enter_context(nc.psum_tensor(f"ssdT{i}", [128, 512], F32)) for i in range(2)]

        bld = Builder(nc, stack)
        sDW = bld.stream("sDW", 16)
        sDS = bld.stream("sDS", 16)
        sPE = bld.stream("sPE", 1)
        sV = bld.stream("sV", 1)
        sSC = bld.stream("sSC", 1)
        sGP = bld.stream("sGP", 1)
        em = bld.emit

        bkA, bkB = Bank(), Bank()
        bkP = [Bank(), Bank()]
        bkT = [Bank(), Bank()]
        b_winsl = [Bank(), Bank()]
        b_wout = [Bank(), Bank()]
        b_wg = [Bank(), Bank()]
        b_wu = [Bank(), Bank()]
        b_wd = [Bank(), Bank()]
        b_cacc = [Bank(), Bank()]
        b_ures = [Bank(), Bank()]
        b_lnxb = [Bank(), Bank()]
        b_lnsq = [Bank(), Bank()]
        b_sil4 = [Bank() for _ in range(4)]
        b_Ls = [Bank(), Bank()]
        b_Ms4 = [Bank() for _ in range(4)]
        b_wb = [Bank(), Bank()]
        b_ct4 = [Bank() for _ in range(4)]
        M_tok = [None] * 4
        C_tok = [None] * 4
        b_xt = [Bank(), Bank()]
        b_bt = [Bank(), Bank()]
        b_yr = [Bank(), Bank()]
        b_ubf = Bank()
        b_atm = Bank()
        b_negptm = Bank()
        b_rows2 = Bank()
        b_wstate = Bank()
        b_lamb = Bank()
        b_lnm = Bank()
        b_lnmsq = Bank()
        b_lnvar = Bank()
        b_lnm2 = Bank()
        b_lnrstd = Bank()
        b_mb = Bank()

        t_id = em("sync", lambda e: e.dma_start(identf[:], identf_in), inc=sDS)
        t_idb = em("sync", lambda e: e.dma_start(identbf[:], identbf_in), inc=sDS)
        t_tri = em("sync", lambda e: e.dma_start(triu[:], triu_in), inc=sDS)
        t_nm = em("sync", lambda e: e.dma_start(negmask[:], negmask_in), inc=sDS)
        t_cw = em("sync", lambda e: e.dma_start(convw[:], convw_in), inc=sDS)
        t_cb = em("sync", lambda e: e.dma_start(convb[:], convb_in), inc=sDS)
        t_zb = em("sync", lambda e: e.dma_start(zbias[:], zb_in), inc=sDS)
        t_dr = em("sync", lambda e: e.dma_start(drow[:], drow_in), inc=sDS)
        t_gin = em("sync", lambda e: e.dma_start(gate_c[:], gate_in), inc=sDS)
        t_or = em("sync", lambda e: e.dma_start(onesrow[:], onesrow_in), inc=sDS)
        t_ocb = em("vector", lambda e: e.memset(onescol_bf[:], 1.0), inc=sV)
        t_eps = em("vector", lambda e: e.memset(eps_c[:], EPS), inc=sV)
        t_s0 = em("vector", lambda e: e.memset(S_f[1][:], 0.0), inc=sV)
        hz = {}
        for t in range(48):
            hz[t] = em("vector", lambda e, t=t: e.memset(xbc_tile(t, 0, 3), 0.0), inc=sV)
        t_dmm = em("tensor", lambda e: e.matmul(pairA[:, 0:32], onesrow[:], drow[:],
                                                start=True, stop=True),
                   deps=[t_or, t_dr], inc=sPE)
        t_dbc = em("vector", lambda e: e.tensor_copy(d_bc[:], pairA[:, 0:32]),
                   deps=[t_dmm], inc=sV)
        bkA.wrote(t_dmm)
        bkA.read(t_dbc)

        def load_ubf(src):
            tok = em("gpsimd",
                     lambda e: e.dma_start(
                         u_bf[:].rearrange("p (kc t) -> p kc t", kc=NKC),
                         src.rearrange("(kc p) t -> kc p t", p=128).transpose([1, 0, 2])),
                     deps=b_ubf.wdeps(), inc=sDW)
            b_ubf.wrote(tok)
            return tok

        def slab_src(wT, o0, ow, nkc):
            return wT[:, o0:o0 + ow].rearrange("(kc p) o -> kc p o", p=128).transpose([1, 0, 2])

        pair_banks = [(pairA, bkA), (pairB, bkB)]

        def proj_stage(oclist, u_ready, z_to_silu, extra_evac_deps, tile_tok):
            evtok = {}
            for gi, oc in enumerate(oclist):
                ow = 32 if oc == 64 else 128
                buf, bbuf = winsl[gi % 2], b_winsl[gi % 2]
                t_l = em("gpsimd",
                         lambda e, buf=buf, oc=oc, ow=ow: e.dma_start(
                             buf[:, 0:NKC * ow].rearrange("p (kc o) -> p kc o", kc=NKC),
                             slab_src(w_inT, oc * 128, ow, NKC)),
                         deps=bbuf.wdeps(), inc=sDW)
                bbuf.wrote(t_l)
                pr, bk = pair_banks[gi % 2]
                first_deps = [t_l, u_ready] + bk.wdeps()
                t_mm = None
                for kc in range(NKC):
                    for tt in range(2):
                        last = (kc == NKC - 1 and tt == 1)

                        def mmfn(e, buf=buf, kc=kc, tt=tt, ow=ow, pr=pr):
                            return e.matmul(pr[0:ow, tt * 512:(tt + 1) * 512],
                                            buf[:, kc * ow:(kc + 1) * ow],
                                            u_bf[:, kc * 1024 + tt * 512:kc * 1024 + (tt + 1) * 512],
                                            start=(kc == 0), stop=(kc == NKC - 1))
                        t_mm = em("tensor", mmfn,
                                  deps=(first_deps if (kc == 0 and tt == 0) else ()),
                                  inc=(sPE if last else None))
                bk.wrote(t_mm)
                bbuf.read(t_mm)
                b_ubf.read(t_mm)
                ed = [t_mm] + [d for d in extra_evac_deps(oc) if d is not None]
                if oc == 64:
                    t_e = em("vector", lambda e, pr=pr: e.tensor_copy(A_fm[:], pr[0:32, :]),
                             deps=ed, inc=sV)
                elif oc >= 48 and z_to_silu:
                    j = oc - 48
                    t_e = em("scalar",
                             lambda e, pr=pr, j=j: e.activation(
                                 silu_z[:, j * 1024:(j + 1) * 1024], pr[:], AF.Silu,
                                 bias=zbias[:, j:j + 1], scale=1.0),
                             deps=ed + [t_zb], inc=sSC)
                else:
                    t = oc
                    if gi % 3 != 2:
                        t_e = em("vector", lambda e, pr=pr, t=t: e.tensor_copy(
                            xbc_tile(t, 3, 1027), pr[:]), deps=ed, inc=sV)
                    else:
                        t_e = em("scalar", lambda e, pr=pr, t=t: e.copy(
                            xbc_tile(t, 3, 1027), pr[:]), deps=ed, inc=sSC)
                    if tile_tok is not None:
                        tile_tok[t] = t_e
                bk.read(t_e)
                evtok[oc] = t_e
            return evtok

        def conv_stage(tiles, dep_of_tile, tap3):
            n = 0
            for t in tiles:
                for hh in (1, 0):
                    c0 = hh * 512
                    buf, bb = cacc[n % 2], b_cacc[n % 2]
                    n += 1
                    d0 = list(dep_of_tile(t)) + bb.wdeps() + [t_cw, t_cb]
                    t0 = em("scalar",
                            lambda e, t=t, c0=c0, buf=buf: e.activation(
                                buf[:], xbc_tile(t, c0, c0 + 512), AF.Identity,
                                bias=convb[:, t:t + 1], scale=convw[:, 4 * t:4 * t + 1]),
                            deps=d0, inc=sSC)
                    tprev = t0
                    for k in (1, 2):
                        tprev = em("vector",
                                   lambda e, t=t, c0=c0, k=k, buf=buf: e.scalar_tensor_tensor(
                                       buf[:], xbc_tile(t, c0 + k, c0 + k + 512),
                                       convw[:, 4 * t + k:4 * t + k + 1], buf[:],
                                       ALU.mult, ALU.add),
                                   deps=[tprev], inc=sV)
                    t3 = em("vector",
                            lambda e, t=t, c0=c0, buf=buf: e.scalar_tensor_tensor(
                                xbc_tile(t, c0 + 3, c0 + 515),
                                xbc_tile(t, c0 + 3, c0 + 515),
                                convw[:, 4 * t + 3:4 * t + 4], buf[:],
                                ALU.mult, ALU.add),
                            deps=[tprev], inc=sV)
                    bb.wrote(t3)
                    bb.read(t3)
                    tap3.setdefault(t, []).append(t3)

        def softplus_stage(dep, extra=()):
            t1 = em("vector", lambda e: e.tensor_scalar(a_fm[:], A_fm[:], -1.0, None, ALU.mult),
                    deps=[dep] + list(extra), inc=sV)
            t2 = em("vector", lambda e: e.tensor_tensor(a_fm[:], A_fm[:], a_fm[:], ALU.min),
                    deps=[t1], inc=sV)
            t3 = em("scalar", lambda e: e.activation(a_fm[:], a_fm[:], AF.Exp),
                    deps=[t2], inc=sSC)
            t4 = em("scalar", lambda e: e.activation(a_fm[:], a_fm[:], AF.Ln, bias=1.0),
                    deps=[t3], inc=sSC)
            t5 = em("vector", lambda e: e.tensor_scalar(sp_t2[:], A_fm[:], 0.0, None, ALU.max),
                    deps=[dep], inc=sV)
            t6 = em("vector", lambda e: e.scalar_tensor_tensor(
                a_fm[:], a_fm[:], -1.0, sp_t2[:], ALU.mult, ALU.subtract),
                    deps=[t4, t5], inc=sV)
            return t6

        sbf_tok = {}
        last_prol = [None]

        def ssd_prologue(c, a_dep):
            p0 = ssdP[0]
            d0 = bkP[0].wdeps() + [a_dep, t_id] + b_atm.wdeps()
            tk1 = em("tensor", lambda e, c=c: e.transpose(
                p0[0:128, 0:32], a_fm[:, c * Q:(c + 1) * Q], identf[0:32, 0:32]),
                     deps=d0, inc=sPE)
            last_prol[0] = tk1
            te1 = em("vector", lambda e: e.tensor_copy(a_tm_sb[:], p0[0:128, 0:32]),
                     deps=[tk1], inc=sV)
            b_atm.wrote(te1)
            tk2 = em("tensor", lambda e: e.matmul(p0[:, 32:64], triu[:], a_tm_sb[:],
                                                  start=True, stop=True),
                     deps=[te1, t_tri], inc=sPE)
            b_atm.read(tk2)
            te2 = em("vector", lambda e: e.tensor_scalar(negp_tm[:], p0[:, 32:64],
                                                         -1.0, None, ALU.mult),
                     deps=[tk2] + b_negptm.wdeps(), inc=sV)
            b_negptm.wrote(te2)
            tk3 = em("tensor", lambda e: e.transpose(p0[0:32, 64:192], negp_tm[:], identf[:]),
                     deps=[te2], inc=sPE)
            b_negptm.read(tk3)
            d3 = b_rows2.wdeps()
            te3a = em("vector", lambda e: e.tensor_copy(negp_rows2[:, 0:128], p0[0:32, 64:192]),
                      deps=[tk3] + d3, inc=sV)
            te3b = em("vector", lambda e: e.tensor_copy(negp_rows2[:, 128:256], p0[0:32, 64:192]),
                      deps=[tk3], inc=sV)
            b_rows2.wrote(te3b)
            tk4 = em("tensor", lambda e: e.matmul(
                p0[:, 192:224], negp_rows2[:, 127:128].broadcast_to([32, 128]),
                identf[0:32, 0:32], start=True, stop=True),
                     deps=[te3a, te3b], inc=sPE)
            b_rows2.read(tk4)
            tw1 = em("vector", lambda e: e.tensor_tensor(wslog[:], negp_tm[:], p0[:, 192:224],
                                                         ALU.subtract),
                     deps=[tk4], inc=sV)
            b_negptm.read(tw1)
            dw = b_wstate.wdeps()
            tw2 = em("scalar", lambda e: e.activation(wstate[:], wslog[:], AF.Exp),
                     deps=[tw1] + dw, inc=sSC)
            b_wstate.wrote(tw2)
            dl = b_lamb.wdeps()
            tw3 = em("scalar", lambda e: e.activation(lamb[:], p0[:, 192:224], AF.Exp, scale=-1.0),
                     deps=[tk4, tw1] + dl, inc=sSC)
            b_lamb.wrote(tw3)
            bkP[0].wrote(tk4)
            bkP[0].read(te1)
            bkP[0].read(te2)
            bkP[0].read(te3b)
            bkP[0].read(tw1)
            bkP[0].read(tw3)
            return te3b, te2, tw2, tw3

        def emit_pair_back(ctx):
            (c, pw, j, tev_x, tsn_toks, z_dep, state_dep) = ctx
            par = j % 2
            bt = bkT[par]
            for hh in range(2):
                h = 2 * j + hh
                bb = 64 * hh
                mi = hh * 2 + (j % 2)
                tM = M_tok[mi]
                tC = C_tok[mi]
                sdep = sbf_tok.get((1 - pw, h), state_dep)
                em("tensor", lambda e, hh=hh, bb=bb, par=par, mi=mi: e.matmul(
                    ssdT[par][bb:bb + 64, 128:256], xt_sb[par][:, bb:bb + 64], M_sb4[mi][:],
                    start=True, stop=False, tile_position=(0, bb)),
                   deps=[tM, tev_x] + tsn_toks, inc=None)
                tY2 = em("tensor", lambda e, hh=hh, bb=bb, par=par, pw=pw, h=h, mi=mi: e.matmul(
                    ssdT[par][bb:bb + 64, 128:256],
                    S_bf[1 - pw][bb:bb + 64, (h // 2) * 64:(h // 2) * 64 + 64],
                    ct_sb4[mi][bb:bb + 64, :],
                    start=False, stop=True, tile_position=(bb, bb)),
                         deps=[tC, sdep], inc=sPE)
                b_Ms4[mi].read(tY2)
                b_ct4[mi].read(tY2)
                b_xt[par].read(tY2)
                tYr = em("vector", lambda e, j=j, c=c, hh=hh, bb=bb, par=par, h=h:
                         e.scalar_tensor_tensor(
                             yr_sb[hh][bb:bb + 64, :],
                             xbc_tile(j, 3 + c * Q, 3 + (c + 1) * Q)[bb:bb + 64, :],
                             d_bc[bb:bb + 64, h:h + 1],
                             ssdT[par][bb:bb + 64, 128:256], ALU.mult, ALU.add),
                         deps=[tY2, t_dbc] + b_yr[hh].wdeps(), inc=sV)
                b_yr[hh].wrote(tYr)
                bt.read(tYr)
                tyg = em("vector", lambda e, c=c, hh=hh, bb=bb, h=h: e.tensor_tensor(
                    yg[bb:bb + 64, (h // 2) * 1024 + c * Q:(h // 2) * 1024 + (c + 1) * Q],
                    yr_sb[hh][bb:bb + 64, :],
                    silu_z[bb:bb + 64, (h // 2) * 1024 + c * Q:(h // 2) * 1024 + (c + 1) * Q],
                    ALU.mult),
                          deps=[tYr, z_dep(h // 2)], inc=sV)
                b_yr[hh].read(tyg)

        def ssd_chunks(chunks, glb_off, state_only, a_dep, tile_dep, z_dep,
                       last_pe_tile, state_dep):
            for ci, c in enumerate(chunks):
                cg = glb_off + ci
                pw = cg % 2
                te_rows, te_np, tw_ws, tw_lam = ssd_prologue(c, a_dep)
                prev = None
                for j in range(16):
                    par = j % 2
                    bt = bkT[par]
                    trdeps = bt.wdeps() + list(tile_dep(16 + j)) + list(tile_dep(j)) + [t_idb]
                    tb = em("tensor", lambda e, j=j, c=c, par=par: e.transpose(
                        ssdT[par][:, 0:64].bitcast(BF16),
                        xbc_tile(16 + j, 3 + c * Q, 3 + (c + 1) * Q), identbf[:]),
                            deps=trdeps, inc=sPE)
                    tx = em("tensor", lambda e, j=j, c=c, par=par: e.transpose(
                        ssdT[par][:, 64:128].bitcast(BF16),
                        xbc_tile(j, 3 + c * Q, 3 + (c + 1) * Q), identbf[:]),
                            deps=[], inc=sPE)
                    last_pe_tile[16 + j] = tb
                    last_pe_tile[j] = tx
                    tev_x = em("vector", lambda e, par=par: e.tensor_copy(
                        xt_sb[par][:], ssdT[par][:, 64:128].bitcast(BF16)),
                               deps=[tx] + b_xt[par].wdeps(), inc=sV)
                    b_xt[par].wrote(tev_x)
                    bts = []
                    for hh in range(2):
                        h = 2 * j + hh
                        tbh = em("vector", lambda e, par=par, hh=hh, h=h: e.tensor_scalar(
                            bt_sb[hh][:],
                            ssdT[par][:, 0:64].bitcast(BF16)[:, hh * 64:(hh + 1) * 64],
                            wstate[:, h:h + 1], None, ALU.mult),
                                 deps=[tb, tw_ws] + b_bt[hh].wdeps(), inc=sV)
                        b_bt[hh].wrote(tbh)
                        b_wstate.read(tbh)
                        bts.append(tbh)
                    bt.wrote(tx)
                    bt.read(tev_x)
                    bt.read(bts[0])
                    bt.read(bts[1])
                    if not state_only and SSD_MODE != "state":
                        for hh in range(2):
                            h = 2 * j + hh
                            bb = 64 * hh
                            mi = hh * 2 + (j % 2)
                            seldeps = bkP[hh].wdeps() + [te_rows]
                            em("tensor", lambda e, h=h, hh=hh: e.matmul(
                                ssdP[hh][:, 0:256],
                                identf[0:32, h:h + 1].broadcast_to([32, 128]),
                                negp_rows2[:], start=True, stop=False),
                               deps=seldeps, inc=None)
                            t_msk = em("tensor", lambda e, hh=hh: e.matmul(
                                ssdP[hh][:, 128:256], negmask[:], identbf[:],
                                start=False, stop=True),
                                       deps=[t_nm], inc=sPE)
                            b_rows2.read(t_msk)
                            tGt = em("tensor", lambda e, j=j, c=c, hh=hh, bb=bb: e.matmul(
                                ssdP[hh][:, 256:384],
                                xbc_tile(16 + j, 3 + c * Q, 3 + (c + 1) * Q)[bb:bb + 64, :],
                                xbc_tile(32 + j, 3 + c * Q, 3 + (c + 1) * Q)[bb:bb + 64, :],
                                start=True, stop=True),
                                     deps=list(tile_dep(32 + j)), inc=sPE)
                            last_pe_tile[32 + j] = tGt
                            if SSD_MODE == "front_pe":
                                bkP[hh].wrote(tGt)
                                continue
                            _lvl = {"front_L": 1, "front_Lw": 2, "front_Lwm": 3,
                                    "front_all": 4, "full": 4}[SSD_MODE]
                            tL = em("scalar", lambda e, hh=hh, h=h: e.activation(
                                L_sb[hh][:], ssdP[hh][:, 128:256], AF.Exp,
                                bias=negp_tm[:, h:h + 1], scale=-1.0),
                                    deps=[t_msk] + b_Ls[hh].wdeps(), inc=sSC)
                            b_Ls[hh].wrote(tL)
                            b_negptm.read(tL)
                            if _lvl < 2:
                                bkP[hh].wrote(tGt)
                                bkP[hh].read(tL)
                                continue
                            twb = em("scalar", lambda e, hh=hh, bb=bb: e.activation(
                                wb_sb[hh][bb:bb + 64, :], ssdP[hh][bb:bb + 64, 0:128], AF.Exp,
                                scale=-1.0),
                                     deps=[t_msk] + b_wb[hh].wdeps(), inc=sSC)
                            b_wb[hh].wrote(twb)
                            if _lvl < 3:
                                bkP[hh].wrote(tGt)
                                bkP[hh].read(tL)
                                bkP[hh].read(twb)
                                continue
                            tM = em("vector", lambda e, hh=hh, mi=mi: e.tensor_tensor(
                                M_sb4[mi][:], ssdP[hh][:, 256:384], L_sb[hh][:], ALU.mult),
                                    deps=[tGt, tL, twb] + b_Ms4[mi].wdeps(), inc=sV)
                            b_Ms4[mi].wrote(tM)
                            b_Ls[hh].read(tM)
                            if _lvl < 4:
                                M_tok[mi] = tM
                                bkP[hh].wrote(tGt)
                                bkP[hh].read(tL)
                                bkP[hh].read(tM)
                                continue
                            tC = em("vector", lambda e, j=j, c=c, hh=hh, bb=bb, mi=mi:
                                    e.tensor_tensor(
                                        ct_sb4[mi][bb:bb + 64, :],
                                        xbc_tile(32 + j, 3 + c * Q, 3 + (c + 1) * Q)[bb:bb + 64, :],
                                        wb_sb[hh][bb:bb + 64, :], ALU.mult),
                                    deps=[twb] + b_ct4[mi].wdeps(), inc=sV)
                            b_ct4[mi].wrote(tC)
                            b_wb[hh].read(tC)
                            M_tok[mi] = tM
                            C_tok[mi] = tC
                            bkP[hh].wrote(tGt)
                            bkP[hh].read(tL)
                            bkP[hh].read(twb)
                            bkP[hh].read(tM)
                    tsn_toks = []
                    for hh in range(2):
                        h = 2 * j + hh
                        bb = 64 * hh
                        tS = em("tensor", lambda e, hh=hh, bb=bb, par=par: e.matmul(
                            ssdT[par][bb:bb + 64, 256:320], bt_sb[hh][:],
                            xt_sb[par][:, bb:bb + 64],
                            start=True, stop=True, tile_position=(0, bb)),
                                deps=[bts[hh], tev_x], inc=sPE)
                        bt.wrote(tS)
                        b_xt[par].read(tS)
                        b_bt[hh].read(tS)
                        tSn = em("vector", lambda e, hh=hh, bb=bb, par=par, pw=pw, h=h:
                                 e.scalar_tensor_tensor(
                                     S_f[pw][bb:bb + 64, (h // 2) * 64:(h // 2) * 64 + 64],
                                     S_f[1 - pw][bb:bb + 64, (h // 2) * 64:(h // 2) * 64 + 64],
                                     lamb[bb:bb + 64, h:h + 1],
                                     ssdT[par][bb:bb + 64, 256:320], ALU.mult, ALU.add),
                                 deps=[tS], inc=sV)
                        b_lamb.read(tSn)
                        bt.read(tSn)
                        tsn_toks.append(tSn)
                        if not state_only:
                            tSb = em("vector", lambda e, hh=hh, bb=bb, pw=pw, h=h: e.tensor_copy(
                                S_bf[pw][bb:bb + 64, (h // 2) * 64:(h // 2) * 64 + 64],
                                S_f[pw][bb:bb + 64, (h // 2) * 64:(h // 2) * 64 + 64]),
                                     deps=[tSn], inc=sV)
                            sbf_tok[(pw, h)] = tSb
                    if prev is not None and not state_only and SSD_MODE == "full":
                        emit_pair_back(prev)
                    prev = (c, pw, j, tev_x, tsn_toks, z_dep, state_dep)
                if not state_only and SSD_MODE == "full":
                    emit_pair_back(prev)

        t_upre = load_ubf(u_pre)
        evp = proj_stage(list(range(48)) + [64], t_upre, z_to_silu=False,
                         extra_evac_deps=lambda oc: [hz.get(oc)], tile_tok=None)
        halo_toks = {}
        for t in range(48):
            halo_toks[t] = em("vector", lambda e, t=t: e.tensor_copy(
                halo_sb[:, 3 * t:3 * t + 3], xbc_tile(t, 1024, 1027)),
                deps=[evp[t]], inc=sV)
        t_a_pre = softplus_stage(evp[64])
        tap3_pre = {}
        conv_stage([t for j in range(16) for t in (j, 16 + j)],
                   lambda t: [evp[t], halo_toks[t], hz[t]], tap3_pre)
        last_pe_tile = {}

        def pre_tile_dep(t):
            d = list(tap3_pre.get(t, []))
            return d if d else [evp[t]]

        ssd_chunks(list(range(NCHUNK)), 0, True, t_a_pre, pre_tile_dep, None,
                   last_pe_tile, t_s0)
        fr = bld.frontier()
        t_gs = em("vector", lambda e: e.tensor_scalar(S_f[1][:], S_f[1][:], gate_c[:],
                                                      None, ALU.mult),
                  deps=fr + [t_gin], inc=sV)
        t_gsb = em("vector", lambda e: e.tensor_copy(S_bf[1][:], S_f[1][:]),
                   deps=[t_gs], inc=sV)

        def trunc_exit():
            frX = bld.frontier()
            lastt = None
            for o in range(16):
                lastt = em("sync", lambda e, o=o: e.dma_start(out_r[:, o, :], x1v(o)),
                           deps=frX, inc=sDS)
            em("sync", lambda e: e.nop(), deps=[lastt], inc=None)

        out_r = out_feat.rearrange("(o p) t -> o p t", p=128).transpose([1, 0, 2])
        if TRUNCATE == "preSSD":
            trunc_exit()
            with nc.Block() as block:
                bld.flush(block)
            return nc

        t_uown = load_ubf(u_own)
        main_tiles = {}

        def main_evac_deps(oc):
            d = []
            if oc < 48:
                d.append(last_pe_tile.get(oc))
                d += tap3_pre.get(oc, [])
                d.append(halo_toks[oc])
            return d

        evm = proj_stage(list(range(48)) + [64] + list(range(48, 64)), t_uown,
                         z_to_silu=True, extra_evac_deps=main_evac_deps,
                         tile_tok=main_tiles)
        halow = {}
        for t in range(48):
            halow[t] = em("vector", lambda e, t=t: e.tensor_scalar(
                xbc_tile(t, 0, 3), halo_sb[:, 3 * t:3 * t + 3], gate_c[:], None, ALU.mult),
                deps=[evm[t], t_gin] + tap3_pre.get(t, [halo_toks[t]]), inc=sV)
        t_a_main = softplus_stage(evm[64], extra=[last_prol[0]])
        tap3_main = {}
        conv_stage([t for j in range(16) for t in (j, 16 + j, 32 + j)],
                   lambda t: [evm[t], halow[t]], tap3_main)
        if TRUNCATE == "postA":
            trunc_exit()
            with nc.Block() as block:
                bld.flush(block)
            return nc

        last_pe_tile2 = {}
        ssd_chunks(list(range(NCHUNK)), NCHUNK, False, t_a_main,
                   lambda t: list(tap3_main.get(t, [])),
                   lambda zt: evm[48 + zt], last_pe_tile2, t_gsb)

        if TRUNCATE == "postB":
            trunc_exit()
            with nc.Block() as block:
                bld.flush(block)
            return nc

        if DEBUG:
            dbg_xbc = nc.dram_tensor("dbg_xbc", [128, 48 * 1027], BF16,
                                     kind="ExternalOutput").ap()
            dbg_a = nc.dram_tensor("dbg_a", [32, 1024], F32, kind="ExternalOutput").ap()
            dbg_silu = nc.dram_tensor("dbg_silu", [128, 16 * 1024], BF16,
                                      kind="ExternalOutput").ap()
            dbg_yg = nc.dram_tensor("dbg_yg", [128, 16 * 1024], BF16,
                                    kind="ExternalOutput").ap()
            dbg_S = nc.dram_tensor("dbg_S", [128, 2048], F32, kind="ExternalOutput").ap()
            frD = bld.frontier()
            em("sync", lambda e: e.dma_start(dbg_xbc, xbc[:]), deps=frD, inc=sDS)
            em("sync", lambda e: e.dma_start(dbg_a, a_fm[:]), deps=frD, inc=sDS)
            em("sync", lambda e: e.dma_start(dbg_silu, silu_z[:]), deps=frD, inc=sDS)
            em("sync", lambda e: e.dma_start(dbg_yg, yg[:]), deps=frD, inc=sDS)
            em("sync", lambda e: e.dma_start(dbg_S[:, 0:1024], S_f[0][:]), deps=frD, inc=sDS)
            em("sync", lambda e: e.dma_start(dbg_S[:, 1024:2048], S_f[1][:]), deps=frD, inc=sDS)

        frB = bld.frontier()
        stat_banks = [ssdP[0], ssdP[1], ssdT[0], ssdT[1]]
        stat_bk = [bkP[0], bkP[1], bkT[0], bkT[1]]
        x1tok = {}
        statmm = {}
        for o in range(16):
            buf, bbuf = woutsl[o % 2], b_wout[o % 2]
            t_l = em("gpsimd", lambda e, buf=buf, o=o: e.dma_start(
                buf[:].rearrange("p (kc o) -> p kc o", kc=NKC),
                slab_src(w_outT, o * 128, 128, NKC)),
                     deps=bbuf.wdeps() + (frB if o < 2 else []), inc=sDW)
            bbuf.wrote(t_l)
            t_u = em("sync", lambda e, o=o: e.dma_start(
                ures[o % 2][:], u_own[o * 128:(o + 1) * 128, :]),
                     deps=b_ures[o % 2].wdeps() + (frB if o < 2 else []), inc=sDS)
            b_ures[o % 2].wrote(t_u)
            pr, bk = pair_banks[o % 2]
            first_deps = [t_l] + bk.wdeps() + (frB if o < 2 else [])
            t_mm = None
            for kc in range(NKC):
                for tt in range(2):
                    last = (kc == NKC - 1 and tt == 1)

                    def mmfn(e, buf=buf, kc=kc, tt=tt, pr=pr):
                        return e.matmul(pr[:, tt * 512:(tt + 1) * 512],
                                        buf[:, kc * 128:(kc + 1) * 128],
                                        yg[:, kc * 1024 + tt * 512:kc * 1024 + (tt + 1) * 512],
                                        start=(kc == 0), stop=(kc == NKC - 1))
                    t_mm = em("tensor", mmfn, deps=(first_deps if (kc == 0 and tt == 0) else ()),
                              inc=(sPE if last else None))
            bk.wrote(t_mm)
            bbuf.read(t_mm)
            t_e = em("vector", lambda e, o=o, pr=pr: e.tensor_tensor(
                x1v(o), ures[o % 2][:], pr[:], ALU.add),
                     deps=[t_mm, t_u] + (frB if o < 2 else []), inc=sV)
            b_ures[o % 2].read(t_e)
            bk.read(t_e)
            x1tok[o] = t_e
            t_xb = em("vector", lambda e, o=o: e.tensor_copy(lnxb[o % 2][:], x1v(o)),
                      deps=[t_e] + b_lnxb[o % 2].wdeps(), inc=sV)
            b_lnxb[o % 2].wrote(t_xb)
            t_sq = em("scalar", lambda e, o=o: e.activation(lnsq[o % 2][:], x1v(o), AF.Square),
                      deps=[t_e] + b_lnsq[o % 2].wdeps(), inc=sSC)
            b_lnsq[o % 2].wrote(t_sq)
            for tt in range(2):
                tmm1 = em("tensor", lambda e, o=o, tt=tt: e.matmul(
                    stat_banks[tt][0:1, 0:512], onescol_bf[:],
                    lnxb[o % 2][:, tt * 512:(tt + 1) * 512],
                    start=(o == 0), stop=(o == 15)),
                          deps=[t_xb, t_ocb] + (stat_bk[tt].wdeps() if o == 0 else []),
                          inc=sPE)
                tmm2 = em("tensor", lambda e, o=o, tt=tt: e.matmul(
                    stat_banks[2 + tt][0:1, 0:512], onescol_bf[:],
                    lnsq[o % 2][:, tt * 512:(tt + 1) * 512],
                    start=(o == 0), stop=(o == 15)),
                          deps=[t_sq] + (stat_bk[2 + tt].wdeps() if o == 0 else []),
                          inc=sPE)
                if o == 15:
                    statmm[tt] = (tmm1, tmm2)
                b_lnxb[o % 2].read(tmm1)
                b_lnsq[o % 2].read(tmm2)
                if o == 15:
                    stat_bk[tt].wrote(tmm1)
                    stat_bk[2 + tt].wrote(tmm2)

        def ln_finalize_apply(x1dep, statmm, out_bf):
            toks = {}
            for tt in range(2):
                tsum, tsq = statmm[tt]
                tm = em("vector", lambda e, tt=tt: e.tensor_scalar(
                    ln_m[:], stat_banks[tt][0:1, 0:512], 1.0 / DM, None, ALU.mult),
                        deps=[tsum] + b_lnm.wdeps(), inc=sV)
                b_lnm.wrote(tm)
                stat_bk[tt].read(tm)
                tmsq = em("vector", lambda e, tt=tt: e.tensor_scalar(
                    ln_msq[:], stat_banks[2 + tt][0:1, 0:512], 1.0 / DM, None, ALU.mult),
                          deps=[tsq] + b_lnmsq.wdeps(), inc=sV)
                b_lnmsq.wrote(tmsq)
                stat_bk[2 + tt].read(tmsq)
                tm2 = em("scalar", lambda e: e.activation(ln_m2[:], ln_m[:], AF.Square),
                         deps=[tm] + b_lnm2.wdeps(), inc=sSC)
                b_lnm2.wrote(tm2)
                b_lnm.read(tm2)
                tv = em("vector", lambda e: e.tensor_tensor(ln_var[:], ln_msq[:], ln_m2[:],
                                                            ALU.subtract),
                        deps=[tmsq, tm2] + b_lnvar.wdeps(), inc=sV)
                b_lnvar.wrote(tv)
                b_lnmsq.read(tv)
                b_lnm2.read(tv)
                tl = em("scalar", lambda e: e.activation(ln_rstd[:], ln_var[:], AF.Ln,
                                                         bias=eps_c[:]),
                        deps=[tv, t_eps] + b_lnrstd.wdeps(), inc=sSC)
                tr = em("scalar", lambda e: e.activation(ln_rstd[:], ln_rstd[:], AF.Exp,
                                                         scale=-0.5),
                        deps=[tl], inc=sSC)
                b_lnrstd.wrote(tr)
                b_lnvar.read(tl)
                dA2 = bkA.wdeps()
                tbm = em("tensor", lambda e: e.matmul(pairA[:, 0:512], onesrow[:], ln_m[:],
                                                      start=True, stop=True),
                         deps=[tm, t_or] + dA2, inc=sPE)
                b_lnm.read(tbm)
                tbr = em("tensor", lambda e: e.matmul(pairA[:, 512:1024], onesrow[:], ln_rstd[:],
                                                      start=True, stop=True),
                         deps=[tr], inc=sPE)
                b_lnrstd.read(tbr)
                bkA.wrote(tbr)
                tmb = em("scalar", lambda e: e.copy(mb_sb[:], pairA[:, 0:512]),
                         deps=[tbm] + b_mb.wdeps(), inc=sSC)
                b_mb.wrote(tmb)
                bkA.read(tmb)
                for o in range(16):
                    t1 = em("vector", lambda e, o=o, tt=tt: e.tensor_tensor(
                        x1v(o)[:, tt * 512:(tt + 1) * 512],
                        x1v(o)[:, tt * 512:(tt + 1) * 512], mb_sb[:], ALU.subtract),
                            deps=[tmb, x1dep(o, tt)], inc=sV)
                    b_mb.read(t1)
                    t2 = em("vector", lambda e, o=o, tt=tt: e.tensor_tensor(
                        x1v(o)[:, tt * 512:(tt + 1) * 512],
                        x1v(o)[:, tt * 512:(tt + 1) * 512], pairA[:, 512:1024], ALU.mult),
                            deps=[t1, tbr], inc=sV)
                    bkA.read(t2)
                    tok = t2
                    if out_bf is not None:
                        tok = em("vector", lambda e, o=o, tt=tt: e.tensor_copy(
                            out_bf[:, o * 1024 + tt * 512:o * 1024 + (tt + 1) * 512],
                            x1v(o)[:, tt * 512:(tt + 1) * 512]),
                                 deps=[t2], inc=sV)
                    toks[(o, tt)] = tok
            return toks

        ln1 = ln_finalize_apply(lambda o, tt: x1tok[o], statmm, x1_bf)

        if DEBUG:
            dbg_x1 = nc.dram_tensor("dbg_x1", [128, 16 * 1024], F32,
                                    kind="ExternalOutput").ap()
            for o in range(16):
                em("sync", lambda e, o=o: e.dma_start(dbg_x1[:, o * 1024:(o + 1) * 1024],
                                                      x1v(o)),
                   deps=[ln1[(o, 0)], ln1[(o, 1)]], inc=sDS)

        if TRUNCATE == "postC":
            trunc_exit()
            with nc.Block() as block:
                bld.flush(block)
            return nc

        frC = bld.frontier()
        down_done = {}
        gate_banks = [(pairA, pairB, bkA, bkB),
                      (ssdP[0], ssdT[0], bkP[0], bkT[0])]
        lastln = [ln1[(15, 0)], ln1[(15, 1)]]
        for hb in range(2):
            gtoks = {}
            for m_ in range(32):
                m = hb * 32 + m_
                ex = frC if (hb == 0 and m_ < 2) else []
                bg, bu = b_wg[m_ % 2], b_wu[m_ % 2]
                t_lg = em("gpsimd", lambda e, m=m, m_=m_: e.dma_start(
                    wgsl[m_ % 2][:].rearrange("p (kc o) -> p kc o", kc=NKC),
                    slab_src(wgT, m * 128, 128, NKC)),
                          deps=bg.wdeps() + ex, inc=sDW)
                bg.wrote(t_lg)
                t_lu = em("gpsimd", lambda e, m=m, m_=m_: e.dma_start(
                    wusl[m_ % 2][:].rearrange("p (kc o) -> p kc o", kc=NKC),
                    slab_src(wuT, m * 128, 128, NKC)),
                          deps=bu.wdeps(), inc=sDW)
                bu.wrote(t_lu)
                if m_ % 2 == 0:
                    prg, pru, bkg, bku = pairA, pairB, bkA, bkB
                    gv = lambda tt: prg[:, tt * 512:(tt + 1) * 512]
                    uv = lambda tt: pru[:, tt * 512:(tt + 1) * 512]
                else:
                    gv = lambda tt: [ssdP[0], ssdP[1]][tt][:, 0:512]
                    uv = lambda tt: [ssdT[0], ssdT[1]][tt][:, 0:512]
                    bkg, bku = None, None
                if m_ % 2 == 0:
                    bdeps = bkA.wdeps() + bkB.wdeps()
                else:
                    bdeps = (bkP[0].wdeps() + bkP[1].wdeps()
                             + bkT[0].wdeps() + bkT[1].wdeps())
                first_deps = [t_lg, t_lu] + lastln + bdeps + ex
                t_mm = None
                for kc in range(NKC):
                    for tt in range(2):
                        for gu in range(2):
                            last = (kc == NKC - 1 and tt == 1 and gu == 1)

                            def mmfn(e, m_=m_, kc=kc, tt=tt, gu=gu, gv=gv, uv=uv):
                                w = (wgsl if gu == 0 else wusl)[m_ % 2]
                                pv = (gv if gu == 0 else uv)(tt)
                                return e.matmul(
                                    pv, w[:, kc * 128:(kc + 1) * 128],
                                    x1_bf[:, kc * 1024 + tt * 512:kc * 1024 + (tt + 1) * 512],
                                    start=(kc == 0), stop=(kc == NKC - 1))
                            t_mm = em("tensor", mmfn,
                                      deps=(first_deps if (kc == 0 and tt == 0 and gu == 0) else ()),
                                      inc=(sPE if last else None))
                bg.read(t_mm)
                bu.read(t_mm)
                gts = []
                for tt in range(2):
                    si = tt * 2 + m_ % 2
                    t_sg = em("scalar", lambda e, tt=tt, si=si, gv=gv: e.activation(
                        silu4[si][:], gv(tt), AF.Silu),
                              deps=[t_mm] + b_sil4[si].wdeps() + ex, inc=sSC)
                    b_sil4[si].wrote(t_sg)
                    t_gt = em("vector", lambda e, m_=m_, tt=tt, si=si, uv=uv: e.tensor_tensor(
                        gated[:, m_ * 1024 + tt * 512:m_ * 1024 + (tt + 1) * 512],
                        silu4[si][:], uv(tt), ALU.mult),
                              deps=[t_sg] + ex, inc=sV)
                    b_sil4[si].read(t_gt)
                    gts.append(t_gt)
                if m_ % 2 == 0:
                    bkA.wrote(t_mm); bkB.wrote(t_mm)
                    bkA.read(gts[0]); bkA.read(gts[1])
                    bkB.read(gts[0]); bkB.read(gts[1])
                else:
                    for bx, gt in ((bkP[0], gts[0]), (bkP[1], gts[1])):
                        bx.wrote(t_mm); bx.read(gt)
                    for bx in (bkT[0], bkT[1]):
                        bx.wrote(t_mm); bx.read(gts[0]); bx.read(gts[1])
                gtoks[m_] = gts
            glast = gtoks[31][1]
            for o in range(16):
                bw = b_wd[o % 2]
                t_ld = em("gpsimd", lambda e, o=o, hb=hb: e.dma_start(
                    wdsl[o % 2][:].rearrange("p (kc o) -> p kc o", kc=32),
                    wdT[hb * 4096:(hb + 1) * 4096, o * 128:(o + 1) * 128]
                    .rearrange("(kc p) o -> kc p o", p=128).transpose([1, 0, 2])),
                          deps=bw.wdeps() + (frC if (hb == 0 and o == 0) else []), inc=sDW)
                bw.wrote(t_ld)
                pr, bk = (pairA, bkA) if o % 2 == 0 else (pairB, bkB)
                t_mm = None
                for kc in range(32):
                    for tt in range(2):
                        last = (kc == 31 and tt == 1)
                        if kc == 0 and tt == 0:
                            deps0 = bk.wdeps() + [t_ld, glast]
                        else:
                            deps0 = ()

                        def mmfn(e, o=o, kc=kc, tt=tt, pr=pr):
                            return e.matmul(
                                pr[:, tt * 512:(tt + 1) * 512],
                                wdsl[o % 2][:, kc * 128:(kc + 1) * 128],
                                gated[:, kc * 1024 + tt * 512:kc * 1024 + (tt + 1) * 512],
                                start=(kc == 0), stop=(kc == 31))
                        t_mm = em("tensor", mmfn, deps=deps0, inc=(sPE if last else None))
                bk.wrote(t_mm)
                bw.read(t_mm)
                for tt in range(2):
                    t_add = em("vector", lambda e, o=o, tt=tt, pr=pr: e.tensor_tensor(
                        x1v(o)[:, tt * 512:(tt + 1) * 512],
                        x1v(o)[:, tt * 512:(tt + 1) * 512], pr[:, tt * 512:(tt + 1) * 512],
                        ALU.add),
                               deps=[t_mm, ln1[(o, tt)]]
                               + ([down_done[(o, tt)]] if hb == 1 else []), inc=sV)
                    bk.read(t_add)
                    down_done[(o, tt)] = t_add

        frD2 = bld.frontier()
        statmm2 = {}
        for o in range(16):
            dd = [down_done[(o, 0)], down_done[(o, 1)]] + (frD2 if o == 0 else [])
            t_xb = em("vector", lambda e, o=o: e.tensor_copy(lnxb[o % 2][:], x1v(o)),
                      deps=dd + b_lnxb[o % 2].wdeps(), inc=sV)
            b_lnxb[o % 2].wrote(t_xb)
            t_sq = em("scalar", lambda e, o=o: e.activation(lnsq[o % 2][:], x1v(o), AF.Square),
                      deps=dd + b_lnsq[o % 2].wdeps(), inc=sSC)
            b_lnsq[o % 2].wrote(t_sq)
            for tt in range(2):
                tmm1 = em("tensor", lambda e, o=o, tt=tt: e.matmul(
                    stat_banks[tt][0:1, 0:512], onescol_bf[:],
                    lnxb[o % 2][:, tt * 512:(tt + 1) * 512],
                    start=(o == 0), stop=(o == 15)),
                          deps=[t_xb] + (stat_bk[tt].wdeps() if o == 0 else []),
                          inc=sPE)
                tmm2 = em("tensor", lambda e, o=o, tt=tt: e.matmul(
                    stat_banks[2 + tt][0:1, 0:512], onescol_bf[:],
                    lnsq[o % 2][:, tt * 512:(tt + 1) * 512],
                    start=(o == 0), stop=(o == 15)),
                          deps=[t_sq] + (stat_bk[2 + tt].wdeps() if o == 0 else []),
                          inc=sPE)
                if o == 15:
                    statmm2[tt] = (tmm1, tmm2)
                    stat_bk[tt].wrote(tmm1)
                    stat_bk[2 + tt].wrote(tmm2)
                b_lnxb[o % 2].read(tmm1)
                b_lnsq[o % 2].read(tmm2)
        if DEBUG:
            dbg_st = nc.dram_tensor("dbg_st", [4, 512], F32, kind="ExternalOutput").ap()
            stt = []
            for i in range(4):
                tt_ = i % 2
                src = stat_banks[i // 2 * 2 + tt_] if False else stat_banks[[0, 1, 2, 3][i]]
                tcp = em("vector", lambda e, i=i: e.tensor_copy(
                    mb_sb[0:1, 0:512], stat_banks[i][0:1, 0:512]),
                         deps=[statmm2[0][0], statmm2[0][1], statmm2[1][0], statmm2[1][1]]
                         + b_mb.wdeps(), inc=sV)
                b_mb.wrote(tcp)
                td = em("sync", lambda e, i=i: e.dma_start(dbg_st[i:i + 1, :], mb_sb[0:1, 0:512]),
                        deps=[tcp], inc=sDS)
                b_mb.read(td)
                stt.append(td)
        ln2 = ln_finalize_apply(lambda o, tt: down_done[(o, tt)], statmm2, None)

        last = None
        for o in range(16):
            last = em("sync", lambda e, o=o: e.dma_start(out_r[:, o, :], x1v(o)),
                      deps=[ln2[(o, 0)], ln2[(o, 1)]], inc=sDS)
        em("sync", lambda e: e.nop(), deps=[last], inc=None)

        with nc.Block() as block:
            bld.flush(block)

    return nc


_NC_CACHE = None


def kernel(L0_attn_u, L0_attn_W_in, L0_attn_conv_w, L0_attn_conv_b, L0_attn_z_bias,
           L0_attn_D, L0_attn_W_out, L0_ffn_w_gate, L0_ffn_w_up, L0_ffn_w_down):
    global _NC_CACHE
    u = np.asarray(L0_attn_u, np.float32)
    w_inT = np.ascontiguousarray(np.asarray(L0_attn_W_in, np.float32).T)
    w_outT = np.ascontiguousarray(np.asarray(L0_attn_W_out, np.float32).T)
    wgT = np.ascontiguousarray(np.asarray(L0_ffn_w_gate, np.float32).T)
    wuT = np.ascontiguousarray(np.asarray(L0_ffn_w_up, np.float32).T)
    wdT = np.ascontiguousarray(np.asarray(L0_ffn_w_down, np.float32).T)
    convw = np.ascontiguousarray(
        np.asarray(L0_attn_conv_w, np.float32).reshape(48, 128, 4).transpose(1, 0, 2)
        .reshape(128, 192))
    convb = np.ascontiguousarray(np.asarray(L0_attn_conv_b, np.float32).reshape(48, 128).T)
    zb = np.ascontiguousarray(np.asarray(L0_attn_z_bias, np.float32).reshape(16, 128).T)
    drow = np.ascontiguousarray(np.asarray(L0_attn_D, np.float32).reshape(1, NH))

    identf = np.eye(128, dtype=np.float32)
    identbf = np.eye(128).astype(bf16)
    triu = np.triu(np.ones((128, 128), np.float32))
    ii = np.arange(128)
    negmaskT = np.where(ii[:, None] < ii[None, :], BIG, 0.0).astype(np.float32).astype(bf16)
    onesrow = np.ones((1, 128), np.float32)

    u_feat = np.ascontiguousarray(u.transpose(0, 2, 1))

    if _NC_CACHE is None:
        _NC_CACHE = build_nc()
    nc = _NC_CACHE

    in_maps = []
    for c in range(8):
        b, half = c // 2, c % 2
        own = np.ascontiguousarray(u_feat[b][:, half * 1024:(half + 1) * 1024])
        if half == 1:
            pre = np.ascontiguousarray(u_feat[b][:, 0:1024])
            gate = np.ones((128, 1), np.float32)
        else:
            pre = np.zeros((DM, TOWN), np.float32)
            gate = np.zeros((128, 1), np.float32)
        in_maps.append({
            "u_own": own, "u_pre": pre, "gate": gate,
            "w_inT": w_inT, "w_outT": w_outT, "wgT": wgT, "wuT": wuT, "wdT": wdT,
            "convw": convw, "convb": convb, "zb": zb, "drow": drow,
            "identf": identf, "identbf": identbf, "triu": triu,
            "negmaskT": negmaskT, "onesrow": onesrow,
        })

    res = run_bass_kernel_spmd(nc, in_maps, core_ids=list(range(8)))
    global _LAST_RES
    _LAST_RES = res
    out = np.empty((4, 2048, 2048), np.float32)
    for c in range(8):
        b, half = c // 2, c % 2
        out[b, half * 1024:(half + 1) * 1024, :] = res.results[c]["out_feat"].T
    return out


# version 05
# speedup vs baseline: 1.1306x; 1.1306x over previous
import numpy as np
import ml_dtypes
from contextlib import ExitStack

import concourse.bass as bass
import concourse.mybir as mybir
from concourse.bass_utils import run_bass_kernel_spmd

F32 = mybir.dt.float32
BF16 = mybir.dt.bfloat16
U8 = mybir.dt.uint8
AF = mybir.ActivationFunctionType
ALU = mybir.AluOpType
bf16 = ml_dtypes.bfloat16

DM = 2048
NH = 32
TOWN = 1024
Q = 128
NCHUNK = TOWN // Q
NKC = DM // 128
BIG = 3.0e38
EPS = 1e-5

ENGS = ("tensor", "vector", "scalar", "gpsimd", "sync")
DEBUG = False
TRUNCATE = None
SSD_MODE = "full"
_LAST_RES = None


class Stream:
    __slots__ = ("h", "n", "step", "name")

    def __init__(self, h, name, step):
        self.h = h
        self.n = 0
        self.step = step
        self.name = name


class Builder:
    def __init__(self, nc, stack):
        self.nc = nc
        self.ops = {e: [] for e in ENGS}
        self.streams = []
        self.stack = stack

    def stream(self, name, step=1):
        h = self.stack.enter_context(self.nc.semaphore(name))
        s = Stream(h, name, step)
        self.streams.append(s)
        return s

    def emit(self, eng, fn, deps=(), inc=None):
        tok = None
        if inc is not None:
            inc.n += inc.step
            tok = (inc, inc.n)
        dd = tuple(t for t in deps if t is not None)
        self.ops[eng].append((fn, dd, inc, inc.step if inc else 0))
        return tok

    def frontier(self):
        return [(s, s.n) for s in self.streams if s.n > 0]

    def flush(self, block):
        for e in ENGS:
            ops = self.ops[e]

            def make(ops):
                def f(eng):
                    lastw = {}
                    for fn, deps, st, stp in ops:
                        for (ds, dv) in deps:
                            if dv > 0 and lastw.get(ds.name, 0) < dv:
                                eng.wait_ge(ds.h, dv)
                                lastw[ds.name] = dv
                        ins = fn(eng)
                        if st is not None:
                            ins.then_inc(st.h, stp)
                return f

            getattr(block, e)(make(ops))


class Bank:
    def __init__(self):
        self.wtok = None
        self.rtoks = []

    def wdeps(self):
        d = list(self.rtoks)
        if self.wtok is not None:
            d.append(self.wtok)
        self.rtoks = []
        return d

    def wrote(self, tok):
        self.wtok = tok

    def rdeps(self):
        return [self.wtok] if self.wtok is not None else []

    def read(self, tok):
        if tok is not None:
            self.rtoks.append(tok)


def build_nc():
    nc = bass.Bass("TRN2", target_bir_lowering=False)

    def dt_in(name, shape, dt=F32):
        return nc.dram_tensor(name, list(shape), dt, kind="ExternalInput").ap()

    u_own = dt_in("u_own", (DM, TOWN))
    u_pre = dt_in("u_pre", (DM, TOWN))
    w_inT = dt_in("w_inT", (DM, 8224))
    w_outT = dt_in("w_outT", (2048, DM))
    wgT = dt_in("wgT", (DM, 8192))
    wuT = dt_in("wuT", (DM, 8192))
    wdT = dt_in("wdT", (8192, DM))
    convw_in = dt_in("convw", (128, 192))
    convb_in = dt_in("convb", (128, 48))
    zb_in = dt_in("zb", (128, 16))
    drow_in = dt_in("drow", (1, NH))
    gate_in = dt_in("gate", (128, 1))
    identf_in = dt_in("identf", (128, 128))
    identbf_in = dt_in("identbf", (128, 128), BF16)
    triu_in = dt_in("triu", (128, 128))
    negmask_in = dt_in("negmaskT", (128, 128), BF16)
    onesrow_in = dt_in("onesrow", (1, 128))
    out_feat = nc.dram_tensor("out_feat", [DM, TOWN], F32, kind="ExternalOutput").ap()

    stack = ExitStack()
    with stack:
        LIMIT = 212608
        arena = stack.enter_context(nc.sbuf_tensor("arena", [128, LIMIT], U8))

        def carve(off, rows, cols, dt, p0=0):
            nb = cols * mybir.dt.size(dt)
            assert off % 4 == 0 and off + nb <= LIMIT, (off, nb)
            return arena[p0:p0 + rows, off:off + nb].bitcast(dt)

        identf = carve(0, 128, 128, F32)
        triu = carve(512, 128, 128, F32)
        identbf = carve(1024, 128, 128, BF16)
        negmask = carve(1280, 128, 128, BF16)
        convw = carve(1536, 128, 192, F32)
        convb = carve(2304, 128, 48, F32)
        zbias = carve(2496, 128, 16, F32)
        d_bc = carve(2560, 128, 32, F32)
        gate_c = carve(2688, 128, 1, F32)
        onescol_bf = carve(2692, 128, 1, BF16)
        eps_c = carve(2696, 1, 1, F32)
        onesrow = carve(2752, 1, 128, F32)
        drow = carve(3264, 1, 32, F32)
        S_f = [carve(3392, 128, 1024, F32), carve(7488, 128, 1024, F32)]
        S_bf = [carve(11584, 128, 1024, BF16), carve(13632, 128, 1024, BF16)]
        x1_o15 = carve(3392, 128, 1024, F32)
        woutsl = [carve(7488, 128, 2048, BF16), carve(11584, 128, 2048, BF16)]
        silu4 = [carve(7488 + 1024 * i, 128, 512, BF16) for i in range(4)]
        XBC0 = 15680
        xbc = carve(XBC0, 128, 48 * 1027, BF16)

        def xbc_tile(t, c0, c1):
            return xbc[:, t * 1027 + c0: t * 1027 + c1]

        gated = carve(XBC0, 128, 64 * 512, BF16)
        x1_bf = carve(XBC0 + 65536, 128, 16 * 1024, BF16)
        lnxb = [carve(XBC0, 128, 1024, BF16), carve(XBC0 + 2048, 128, 1024, BF16)]
        lnsq = [carve(XBC0 + 4096, 128, 1024, BF16), carve(XBC0 + 6144, 128, 1024, BF16)]
        ln_m = carve(XBC0 + 8192, 1, 512, F32)
        ln_msq = carve(XBC0 + 10240, 1, 512, F32)
        ln_var = carve(XBC0 + 12288, 1, 512, F32)
        ln_rstd = carve(XBC0 + 14336, 1, 512, F32)
        ln_m2 = carve(XBC0 + 16384, 1, 512, F32)
        mb_sb = carve(XBC0 + 18432, 128, 512, F32)
        ures = [carve(XBC0 + 20480, 128, 1024, F32), carve(XBC0 + 24576, 128, 1024, F32)]
        UBF0 = 114272
        u_bf = carve(UBF0, 128, 16 * 1024, BF16)
        yg = carve(UBF0, 128, 16 * 1024, BF16)
        wgsl = [carve(UBF0, 128, 2048, BF16), carve(UBF0 + 4096, 128, 2048, BF16)]
        wusl = [carve(UBF0 + 8192, 128, 2048, BF16), carve(UBF0 + 12288, 128, 2048, BF16)]
        wdsl = [carve(UBF0 + 16384, 128, 4096, BF16), carve(UBF0 + 24576, 128, 4096, BF16)]
        SIL0 = 147040
        silu_z = carve(SIL0, 128, 16 * 1024, BF16)
        x1_lo = carve(SIL0, 128, 8 * 1024, F32)
        TOP0 = 179808
        winsl = [carve(TOP0, 128, 2048, BF16), carve(TOP0 + 4096, 128, 2048, BF16)]
        SS0 = TOP0 + 8192
        cacc = [carve(SS0, 128, 512, F32), carve(SS0 + 2048, 128, 512, F32)]
        a_tm_sb = carve(SS0 + 4096, 128, 32, F32)
        negp_tm = carve(SS0 + 4224, 128, 32, F32)
        negp_rows2 = carve(SS0 + 4352, 32, 256, F32)
        wstate = carve(SS0 + 5376, 128, 32, F32)
        lamb = carve(SS0 + 5504, 128, 32, F32)
        wslog = carve(SS0 + 5632, 128, 32, F32)
        L_sb = [carve(SS0 + 5760, 128, 128, F32), carve(SS0 + 6272, 128, 128, F32)]
        M_sb4 = [carve(SS0 + 6784 + 256 * i, 128, 128, BF16) for i in range(4)]
        wb_sb = [carve(SS0 + 7808, 128, 128, F32), carve(SS0 + 8320, 128, 128, F32)]
        ct_sb4 = [carve(SS0 + 8832 + 256 * i, 128, 128, BF16) for i in range(4)]
        xt_sb = [carve(SS0 + 9856, 128, 128, BF16), carve(SS0 + 10112, 128, 128, BF16)]
        bt_sb = [carve(SS0 + 10368, 128, 64, BF16), carve(SS0 + 10496, 128, 64, BF16)]
        yr_sb = [carve(SS0 + 10624, 128, 128, F32), carve(SS0 + 11136, 128, 128, F32)]
        halo_sb = carve(SS0 + 11648, 128, 144, BF16)
        A_fm = carve(SS0 + 11936, 32, 1024, F32)
        sp_t2 = carve(SS0 + 16032, 32, 1024, F32)
        a_fm = carve(SS0 + 20128, 32, 1024, F32)
        assert SS0 + 24224 <= LIMIT
        x1_mid = carve(TOP0, 128, 7 * 1024, F32)

        def x1v(o):
            if o < 8:
                return x1_lo[:, o * 1024:(o + 1) * 1024]
            if o < 15:
                return x1_mid[:, (o - 8) * 1024:(o - 7) * 1024]
            return x1_o15

        pairA = stack.enter_context(nc.psum_tensor("pairA", [128, 1024], F32))
        pairB = stack.enter_context(nc.psum_tensor("pairB", [128, 1024], F32))
        ssdP = [stack.enter_context(nc.psum_tensor(f"ssdP{i}", [128, 512], F32)) for i in range(2)]
        ssdT = [stack.enter_context(nc.psum_tensor(f"ssdT{i}", [128, 512], F32)) for i in range(2)]

        bld = Builder(nc, stack)
        sDW = bld.stream("sDW", 16)
        sDS = bld.stream("sDS", 16)
        sPE = bld.stream("sPE", 1)
        sV = bld.stream("sV", 1)
        sSC = bld.stream("sSC", 1)
        sGP = bld.stream("sGP", 1)
        em = bld.emit

        bkA, bkB = Bank(), Bank()
        bkP = [Bank(), Bank()]
        bkT = [Bank(), Bank()]
        b_winsl = [Bank(), Bank()]
        b_wout = [Bank(), Bank()]
        b_wg = [Bank(), Bank()]
        b_wu = [Bank(), Bank()]
        b_wd = [Bank(), Bank()]
        b_cacc = [Bank(), Bank()]
        b_ures = [Bank(), Bank()]
        b_lnxb = [Bank(), Bank()]
        b_lnsq = [Bank(), Bank()]
        b_sil4 = [Bank() for _ in range(4)]
        b_Ls = [Bank(), Bank()]
        b_Ms4 = [Bank() for _ in range(4)]
        b_wb = [Bank(), Bank()]
        b_ct4 = [Bank() for _ in range(4)]
        M_tok = [None] * 4
        C_tok = [None] * 4
        b_xt = [Bank(), Bank()]
        b_bt = [Bank(), Bank()]
        b_yr = [Bank(), Bank()]
        b_ubf = Bank()
        b_atm = Bank()
        b_negptm = Bank()
        b_rows2 = Bank()
        b_wstate = Bank()
        b_lamb = Bank()
        b_lnm = Bank()
        b_lnmsq = Bank()
        b_lnvar = Bank()
        b_lnm2 = Bank()
        b_lnrstd = Bank()
        b_mb = Bank()

        t_id = em("sync", lambda e: e.dma_start(identf[:], identf_in), inc=sDS)
        t_idb = em("sync", lambda e: e.dma_start(identbf[:], identbf_in), inc=sDS)
        t_tri = em("sync", lambda e: e.dma_start(triu[:], triu_in), inc=sDS)
        t_nm = em("sync", lambda e: e.dma_start(negmask[:], negmask_in), inc=sDS)
        t_cw = em("sync", lambda e: e.dma_start(convw[:], convw_in), inc=sDS)
        t_cb = em("sync", lambda e: e.dma_start(convb[:], convb_in), inc=sDS)
        t_zb = em("sync", lambda e: e.dma_start(zbias[:], zb_in), inc=sDS)
        t_dr = em("sync", lambda e: e.dma_start(drow[:], drow_in), inc=sDS)
        t_gin = em("sync", lambda e: e.dma_start(gate_c[:], gate_in), inc=sDS)
        t_or = em("sync", lambda e: e.dma_start(onesrow[:], onesrow_in), inc=sDS)
        t_ocb = em("vector", lambda e: e.memset(onescol_bf[:], 1.0), inc=sV)
        t_eps = em("vector", lambda e: e.memset(eps_c[:], EPS), inc=sV)
        t_s0 = em("vector", lambda e: e.memset(S_f[1][:], 0.0), inc=sV)
        hz = {}
        for t in range(32):
            hz[t] = em("vector", lambda e, t=t: e.memset(xbc_tile(t, 0, 3), 0.0), inc=sV)
        t_dmm = em("tensor", lambda e: e.matmul(pairA[:, 0:32], onesrow[:], drow[:],
                                                start=True, stop=True),
                   deps=[t_or, t_dr], inc=sPE)
        t_dbc = em("vector", lambda e: e.tensor_copy(d_bc[:], pairA[:, 0:32]),
                   deps=[t_dmm], inc=sV)
        bkA.wrote(t_dmm)
        bkA.read(t_dbc)

        def load_ubf(src):
            tok = em("gpsimd",
                     lambda e: e.dma_start(
                         u_bf[:].rearrange("p (kc t) -> p kc t", kc=NKC),
                         src.rearrange("(kc p) t -> kc p t", p=128).transpose([1, 0, 2])),
                     deps=b_ubf.wdeps(), inc=sDW)
            b_ubf.wrote(tok)
            return tok

        def slab_src(wT, o0, ow, nkc):
            return wT[:, o0:o0 + ow].rearrange("(kc p) o -> kc p o", p=128).transpose([1, 0, 2])

        pair_banks = [(pairA, bkA), (pairB, bkB)]

        def proj_stage(oclist, u_ready, z_to_silu, extra_evac_deps, tile_tok,
                       post_tile=None):
            evtok = {}
            for gi, oc in enumerate(oclist):
                ow = 32 if oc == 64 else 128
                buf, bbuf = winsl[gi % 2], b_winsl[gi % 2]
                t_l = em("gpsimd",
                         lambda e, buf=buf, oc=oc, ow=ow: e.dma_start(
                             buf[:, 0:NKC * ow].rearrange("p (kc o) -> p kc o", kc=NKC),
                             slab_src(w_inT, oc * 128, ow, NKC)),
                         deps=bbuf.wdeps(), inc=sDW)
                bbuf.wrote(t_l)
                pr, bk = pair_banks[gi % 2]
                first_deps = [t_l, u_ready] + bk.wdeps()
                t_mm = None
                for kc in range(NKC):
                    for tt in range(2):
                        last = (kc == NKC - 1 and tt == 1)

                        def mmfn(e, buf=buf, kc=kc, tt=tt, ow=ow, pr=pr):
                            return e.matmul(pr[0:ow, tt * 512:(tt + 1) * 512],
                                            buf[:, kc * ow:(kc + 1) * ow],
                                            u_bf[:, kc * 1024 + tt * 512:kc * 1024 + (tt + 1) * 512],
                                            start=(kc == 0), stop=(kc == NKC - 1))
                        t_mm = em("tensor", mmfn,
                                  deps=(first_deps if (kc == 0 and tt == 0) else ()),
                                  inc=(sPE if last else None))
                bk.wrote(t_mm)
                bbuf.read(t_mm)
                b_ubf.read(t_mm)
                ed = [t_mm] + [d for d in extra_evac_deps(oc) if d is not None]
                if oc == 64:
                    t_e = em("vector", lambda e, pr=pr: e.tensor_copy(A_fm[:], pr[0:32, :]),
                             deps=ed, inc=sV)
                elif oc >= 48 and z_to_silu:
                    j = oc - 48
                    t_e = em("scalar",
                             lambda e, pr=pr, j=j: e.activation(
                                 silu_z[:, j * 1024:(j + 1) * 1024], pr[:], AF.Silu,
                                 bias=zbias[:, j:j + 1], scale=1.0),
                             deps=ed + [t_zb], inc=sSC)
                else:
                    t = oc
                    if gi % 3 != 2:
                        t_e = em("vector", lambda e, pr=pr, t=t: e.tensor_copy(
                            xbc_tile(t, 3, 1027), pr[:]), deps=ed, inc=sV)
                    else:
                        t_e = em("scalar", lambda e, pr=pr, t=t: e.copy(
                            xbc_tile(t, 3, 1027), pr[:]), deps=ed, inc=sSC)
                    if tile_tok is not None:
                        tile_tok[t] = t_e
                bk.read(t_e)
                evtok[oc] = t_e
                if post_tile is not None:
                    post_tile(oc, t_e)
            return evtok

        conv_n = [0]

        def conv_tile(t, deps0, tap3):
                for hh in (1, 0):
                    c0 = hh * 512
                    buf, bb = cacc[conv_n[0] % 2], b_cacc[conv_n[0] % 2]
                    conv_n[0] += 1
                    d0 = list(deps0) + bb.wdeps() + [t_cw, t_cb]
                    t0 = em("scalar",
                            lambda e, t=t, c0=c0, buf=buf: e.activation(
                                buf[:], xbc_tile(t, c0, c0 + 512), AF.Identity,
                                bias=convb[:, t:t + 1], scale=convw[:, 4 * t:4 * t + 1]),
                            deps=d0, inc=sSC)
                    tprev = t0
                    for k in (1, 2):
                        tprev = em("vector",
                                   lambda e, t=t, c0=c0, k=k, buf=buf: e.scalar_tensor_tensor(
                                       buf[:], xbc_tile(t, c0 + k, c0 + k + 512),
                                       convw[:, 4 * t + k:4 * t + k + 1], buf[:],
                                       ALU.mult, ALU.add),
                                   deps=[tprev], inc=sV)
                    t3 = em("vector",
                            lambda e, t=t, c0=c0, buf=buf: e.scalar_tensor_tensor(
                                xbc_tile(t, c0 + 3, c0 + 515),
                                xbc_tile(t, c0 + 3, c0 + 515),
                                convw[:, 4 * t + 3:4 * t + 4], buf[:],
                                ALU.mult, ALU.add),
                            deps=[tprev], inc=sV)
                    bb.wrote(t3)
                    bb.read(t3)
                    tap3.setdefault(t, []).append(t3)

        def softplus_stage(dep, extra=()):
            t1 = em("vector", lambda e: e.tensor_scalar(a_fm[:], A_fm[:], -1.0, None, ALU.mult),
                    deps=[dep] + list(extra), inc=sV)
            t2 = em("vector", lambda e: e.tensor_tensor(a_fm[:], A_fm[:], a_fm[:], ALU.min),
                    deps=[t1], inc=sV)
            t3 = em("scalar", lambda e: e.activation(a_fm[:], a_fm[:], AF.Exp),
                    deps=[t2], inc=sSC)
            t4 = em("scalar", lambda e: e.activation(a_fm[:], a_fm[:], AF.Ln, bias=1.0),
                    deps=[t3], inc=sSC)
            t5 = em("vector", lambda e: e.tensor_scalar(sp_t2[:], A_fm[:], 0.0, None, ALU.max),
                    deps=[dep], inc=sV)
            t6 = em("vector", lambda e: e.scalar_tensor_tensor(
                a_fm[:], a_fm[:], -1.0, sp_t2[:], ALU.mult, ALU.subtract),
                    deps=[t4, t5], inc=sV)
            return t6

        sbf_tok = {}
        last_prol = [None]

        def ssd_prologue(c, a_dep):
            p0 = ssdP[0]
            d0 = bkP[0].wdeps() + [a_dep, t_id] + b_atm.wdeps()
            tk1 = em("tensor", lambda e, c=c: e.transpose(
                p0[0:128, 0:32], a_fm[:, c * Q:(c + 1) * Q], identf[0:32, 0:32]),
                     deps=d0, inc=sPE)
            last_prol[0] = tk1
            te1 = em("vector", lambda e: e.tensor_copy(a_tm_sb[:], p0[0:128, 0:32]),
                     deps=[tk1], inc=sV)
            b_atm.wrote(te1)
            tk2 = em("tensor", lambda e: e.matmul(p0[:, 32:64], triu[:], a_tm_sb[:],
                                                  start=True, stop=True),
                     deps=[te1, t_tri], inc=sPE)
            b_atm.read(tk2)
            te2 = em("vector", lambda e: e.tensor_scalar(negp_tm[:], p0[:, 32:64],
                                                         -1.0, None, ALU.mult),
                     deps=[tk2] + b_negptm.wdeps(), inc=sV)
            b_negptm.wrote(te2)
            tk3 = em("tensor", lambda e: e.transpose(p0[0:32, 64:192], negp_tm[:], identf[:]),
                     deps=[te2], inc=sPE)
            b_negptm.read(tk3)
            d3 = b_rows2.wdeps()
            te3a = em("vector", lambda e: e.tensor_copy(negp_rows2[:, 0:128], p0[0:32, 64:192]),
                      deps=[tk3] + d3, inc=sV)
            te3b = em("vector", lambda e: e.tensor_copy(negp_rows2[:, 128:256], p0[0:32, 64:192]),
                      deps=[tk3], inc=sV)
            b_rows2.wrote(te3b)
            tk4 = em("tensor", lambda e: e.matmul(
                p0[:, 192:224], negp_rows2[:, 127:128].broadcast_to([32, 128]),
                identf[0:32, 0:32], start=True, stop=True),
                     deps=[te3a, te3b], inc=sPE)
            b_rows2.read(tk4)
            tw1 = em("vector", lambda e: e.tensor_tensor(wslog[:], negp_tm[:], p0[:, 192:224],
                                                         ALU.subtract),
                     deps=[tk4], inc=sV)
            b_negptm.read(tw1)
            dw = b_wstate.wdeps()
            tw2 = em("scalar", lambda e: e.activation(wstate[:], wslog[:], AF.Exp),
                     deps=[tw1] + dw, inc=sSC)
            b_wstate.wrote(tw2)
            dl = b_lamb.wdeps()
            tw3 = em("scalar", lambda e: e.activation(lamb[:], p0[:, 192:224], AF.Exp, scale=-1.0),
                     deps=[tk4, tw1] + dl, inc=sSC)
            b_lamb.wrote(tw3)
            bkP[0].wrote(tk4)
            bkP[0].read(te1)
            bkP[0].read(te2)
            bkP[0].read(te3b)
            bkP[0].read(tw1)
            bkP[0].read(tw3)
            return te3b, te2, tw2, tw3

        def emit_pair_back(ctx):
            (c, pw, j, tev_x, tsn_toks, z_dep, state_dep) = ctx
            par = j % 2
            bt = bkT[par]
            for hh in range(2):
                h = 2 * j + hh
                bb = 64 * hh
                mi = hh * 2 + (j % 2)
                tM = M_tok[mi]
                tC = C_tok[mi]
                sdep = sbf_tok.get((1 - pw, h), state_dep)
                em("tensor", lambda e, hh=hh, bb=bb, par=par, mi=mi: e.matmul(
                    ssdT[par][bb:bb + 64, 128:256], xt_sb[par][:, bb:bb + 64], M_sb4[mi][:],
                    start=True, stop=False, tile_position=(0, bb)),
                   deps=[tM, tev_x] + tsn_toks, inc=None)
                tY2 = em("tensor", lambda e, hh=hh, bb=bb, par=par, pw=pw, h=h, mi=mi: e.matmul(
                    ssdT[par][bb:bb + 64, 128:256],
                    S_bf[1 - pw][bb:bb + 64, (h // 2) * 64:(h // 2) * 64 + 64],
                    ct_sb4[mi][bb:bb + 64, :],
                    start=False, stop=True, tile_position=(bb, bb)),
                         deps=[tC, sdep], inc=sPE)
                b_Ms4[mi].read(tY2)
                b_ct4[mi].read(tY2)
                b_xt[par].read(tY2)
                tYr = em("vector", lambda e, j=j, c=c, hh=hh, bb=bb, par=par, h=h:
                         e.scalar_tensor_tensor(
                             yr_sb[hh][bb:bb + 64, :],
                             xbc_tile(j, 3 + c * Q, 3 + (c + 1) * Q)[bb:bb + 64, :],
                             d_bc[bb:bb + 64, h:h + 1],
                             ssdT[par][bb:bb + 64, 128:256], ALU.mult, ALU.add),
                         deps=[tY2, t_dbc] + b_yr[hh].wdeps(), inc=sV)
                b_yr[hh].wrote(tYr)
                bt.read(tYr)
                tyg = em("vector", lambda e, c=c, hh=hh, bb=bb, h=h: e.tensor_tensor(
                    yg[bb:bb + 64, (h // 2) * 1024 + c * Q:(h // 2) * 1024 + (c + 1) * Q],
                    yr_sb[hh][bb:bb + 64, :],
                    silu_z[bb:bb + 64, (h // 2) * 1024 + c * Q:(h // 2) * 1024 + (c + 1) * Q],
                    ALU.mult),
                          deps=[tYr, z_dep(h // 2)], inc=sV)
                b_yr[hh].read(tyg)

        def ssd_chunks(chunks, glb_off, state_only, a_dep, tile_dep, z_dep,
                       last_pe_tile, state_dep):
            for ci, c in enumerate(chunks):
                cg = glb_off + ci
                pw = cg % 2
                te_rows, te_np, tw_ws, tw_lam = ssd_prologue(c, a_dep)
                prev = None
                for j in range(16):
                    par = j % 2
                    bt = bkT[par]
                    trdeps = bt.wdeps() + list(tile_dep(16 + j)) + list(tile_dep(j)) + [t_idb]
                    tb = em("tensor", lambda e, j=j, c=c, par=par: e.transpose(
                        ssdT[par][:, 0:64].bitcast(BF16),
                        xbc_tile(16 + j, 3 + c * Q, 3 + (c + 1) * Q), identbf[:]),
                            deps=trdeps, inc=sPE)
                    tx = em("tensor", lambda e, j=j, c=c, par=par: e.transpose(
                        ssdT[par][:, 64:128].bitcast(BF16),
                        xbc_tile(j, 3 + c * Q, 3 + (c + 1) * Q), identbf[:]),
                            deps=[], inc=sPE)
                    last_pe_tile[16 + j] = tb
                    last_pe_tile[j] = tx
                    tev_x = em("scalar", lambda e, par=par: e.copy(
                        xt_sb[par][:], ssdT[par][:, 64:128].bitcast(BF16)),
                               deps=[tx] + b_xt[par].wdeps(), inc=sSC)
                    b_xt[par].wrote(tev_x)
                    bts = []
                    for hh in range(2):
                        h = 2 * j + hh
                        tbh = em("scalar", lambda e, par=par, hh=hh, h=h: e.activation(
                            bt_sb[hh][:],
                            ssdT[par][:, 0:64].bitcast(BF16)[:, hh * 64:(hh + 1) * 64],
                            AF.Identity, scale=wstate[:, h:h + 1]),
                                 deps=[tb, tw_ws] + b_bt[hh].wdeps(), inc=sSC)
                        b_bt[hh].wrote(tbh)
                        b_wstate.read(tbh)
                        bts.append(tbh)
                    bt.wrote(tx)
                    bt.read(tev_x)
                    bt.read(bts[0])
                    bt.read(bts[1])
                    if not state_only and SSD_MODE != "state":
                        for hh in range(2):
                            h = 2 * j + hh
                            bb = 64 * hh
                            mi = hh * 2 + (j % 2)
                            seldeps = bkP[hh].wdeps() + [te_rows]
                            em("tensor", lambda e, h=h, hh=hh: e.matmul(
                                ssdP[hh][:, 0:256],
                                identf[0:32, h:h + 1].broadcast_to([32, 128]),
                                negp_rows2[:], start=True, stop=False),
                               deps=seldeps, inc=None)
                            t_msk = em("tensor", lambda e, hh=hh: e.matmul(
                                ssdP[hh][:, 128:256], negmask[:], identbf[:],
                                start=False, stop=True),
                                       deps=[t_nm], inc=sPE)
                            b_rows2.read(t_msk)
                            tGt = em("tensor", lambda e, j=j, c=c, hh=hh, bb=bb: e.matmul(
                                ssdP[hh][:, 256:384],
                                xbc_tile(16 + j, 3 + c * Q, 3 + (c + 1) * Q)[bb:bb + 64, :],
                                xbc_tile(32 + j, 3 + c * Q, 3 + (c + 1) * Q)[bb:bb + 64, :],
                                start=True, stop=True),
                                     deps=list(tile_dep(32 + j)) + list(tile_dep(16 + j)),
                                     inc=sPE)
                            last_pe_tile[32 + j] = tGt
                            if SSD_MODE == "front_pe":
                                bkP[hh].wrote(tGt)
                                continue
                            _lvl = {"front_L": 1, "front_Lw": 2, "front_Lwm": 3,
                                    "front_all": 4, "full": 4}[SSD_MODE]
                            tL = em("scalar", lambda e, hh=hh, h=h: e.activation(
                                L_sb[hh][:], ssdP[hh][:, 128:256], AF.Exp,
                                bias=negp_tm[:, h:h + 1], scale=-1.0),
                                    deps=[t_msk] + b_Ls[hh].wdeps(), inc=sSC)
                            b_Ls[hh].wrote(tL)
                            b_negptm.read(tL)
                            if _lvl < 2:
                                bkP[hh].wrote(tGt)
                                bkP[hh].read(tL)
                                continue
                            twb = em("scalar", lambda e, hh=hh, bb=bb: e.activation(
                                wb_sb[hh][bb:bb + 64, :], ssdP[hh][bb:bb + 64, 0:128], AF.Exp,
                                scale=-1.0),
                                     deps=[t_msk] + b_wb[hh].wdeps(), inc=sSC)
                            b_wb[hh].wrote(twb)
                            if _lvl < 3:
                                bkP[hh].wrote(tGt)
                                bkP[hh].read(tL)
                                bkP[hh].read(twb)
                                continue
                            tM = em("vector", lambda e, hh=hh, mi=mi: e.tensor_tensor(
                                M_sb4[mi][:], ssdP[hh][:, 256:384], L_sb[hh][:], ALU.mult),
                                    deps=[tGt, tL, twb] + b_Ms4[mi].wdeps(), inc=sV)
                            b_Ms4[mi].wrote(tM)
                            b_Ls[hh].read(tM)
                            if _lvl < 4:
                                M_tok[mi] = tM
                                bkP[hh].wrote(tGt)
                                bkP[hh].read(tL)
                                bkP[hh].read(tM)
                                continue
                            tC = em("vector", lambda e, j=j, c=c, hh=hh, bb=bb, mi=mi:
                                    e.tensor_tensor(
                                        ct_sb4[mi][bb:bb + 64, :],
                                        xbc_tile(32 + j, 3 + c * Q, 3 + (c + 1) * Q)[bb:bb + 64, :],
                                        wb_sb[hh][bb:bb + 64, :], ALU.mult),
                                    deps=[twb] + b_ct4[mi].wdeps(), inc=sV)
                            b_ct4[mi].wrote(tC)
                            b_wb[hh].read(tC)
                            M_tok[mi] = tM
                            C_tok[mi] = tC
                            bkP[hh].wrote(tGt)
                            bkP[hh].read(tL)
                            bkP[hh].read(twb)
                            bkP[hh].read(tM)
                    skip_state = (not state_only) and (ci == len(chunks) - 1)
                    tsn_toks = []
                    for hh in range(2 if not skip_state else 0):
                        h = 2 * j + hh
                        bb = 64 * hh
                        tS = em("tensor", lambda e, hh=hh, bb=bb, par=par: e.matmul(
                            ssdT[par][bb:bb + 64, 256:320], bt_sb[hh][:],
                            xt_sb[par][:, bb:bb + 64],
                            start=True, stop=True, tile_position=(0, bb)),
                                deps=[bts[hh], tev_x], inc=sPE)
                        bt.wrote(tS)
                        b_xt[par].read(tS)
                        b_bt[hh].read(tS)
                        tSn = em("vector", lambda e, hh=hh, bb=bb, par=par, pw=pw, h=h:
                                 e.scalar_tensor_tensor(
                                     S_f[pw][bb:bb + 64, (h // 2) * 64:(h // 2) * 64 + 64],
                                     S_f[1 - pw][bb:bb + 64, (h // 2) * 64:(h // 2) * 64 + 64],
                                     lamb[bb:bb + 64, h:h + 1],
                                     ssdT[par][bb:bb + 64, 256:320], ALU.mult, ALU.add),
                                 deps=[tS], inc=sV)
                        b_lamb.read(tSn)
                        bt.read(tSn)
                        tsn_toks.append(tSn)
                        if not state_only:
                            tSb = em("scalar", lambda e, hh=hh, bb=bb, pw=pw, h=h: e.copy(
                                S_bf[pw][bb:bb + 64, (h // 2) * 64:(h // 2) * 64 + 64],
                                S_f[pw][bb:bb + 64, (h // 2) * 64:(h // 2) * 64 + 64]),
                                     deps=[tSn], inc=sSC)
                            sbf_tok[(pw, h)] = tSb
                    if prev is not None and not state_only and SSD_MODE == "full":
                        emit_pair_back(prev)
                    prev = (c, pw, j, tev_x, tsn_toks, z_dep, state_dep)
                if not state_only and SSD_MODE == "full":
                    emit_pair_back(prev)

        t_upre = load_ubf(u_pre)
        halo_toks = {}
        tap3_pre = {}
        t_a_pre_box = [None]

        def pre_post(oc, t_e):
            if oc == 64:
                t_a_pre_box[0] = softplus_stage(t_e)
                return
            t = oc
            halo_toks[t] = em("vector", lambda e, t=t: e.tensor_copy(
                halo_sb[:, 3 * t:3 * t + 3], xbc_tile(t, 1024, 1027)),
                deps=[t_e], inc=sV)
            conv_tile(t, [t_e, halo_toks[t], hz[t]], tap3_pre)

        evp = proj_stage([64] + list(range(32)), t_upre, z_to_silu=False,
                         extra_evac_deps=lambda oc: [hz.get(oc)], tile_tok=None,
                         post_tile=pre_post)
        t_a_pre = t_a_pre_box[0]

        hdeps0 = bkP[1].wdeps()
        halo_c_mm = None
        for t in range(32, 48):
            buf, bbuf = winsl[t % 2], b_winsl[t % 2]
            t_l = em("gpsimd", lambda e, buf=buf, t=t: e.dma_start(
                buf[:].rearrange("p (kc o) -> p kc o", kc=NKC),
                slab_src(w_inT, t * 128, 128, NKC)),
                     deps=bbuf.wdeps(), inc=sDW)
            bbuf.wrote(t_l)
            o3 = 256 + 3 * (t - 32)
            t_mm = None
            for kc in range(NKC):
                def mmfn(e, buf=buf, kc=kc, o3=o3):
                    return e.matmul(ssdP[1][:, o3:o3 + 3],
                                    buf[:, kc * 128:(kc + 1) * 128],
                                    u_bf[:, kc * 1024 + 1021:kc * 1024 + 1024],
                                    start=(kc == 0), stop=(kc == NKC - 1))
                t_mm = em("tensor", mmfn,
                          deps=(([t_l, t_upre] + (hdeps0 if t == 32 else []))
                                if kc == 0 else ()),
                          inc=(sPE if kc == NKC - 1 else None))
            bbuf.read(t_mm)
            b_ubf.read(t_mm)
            halo_c_mm = t_mm
        bkP[1].wrote(halo_c_mm)
        t_haloc = em("vector", lambda e: e.tensor_copy(halo_sb[:, 96:144],
                                                       ssdP[1][:, 256:304]),
                     deps=[halo_c_mm], inc=sV)
        bkP[1].read(t_haloc)
        last_pe_tile = {}

        def pre_tile_dep(t):
            d = list(tap3_pre.get(t, []))
            return d if d else [evp[t]]

        ssd_chunks(list(range(NCHUNK)), 0, True, t_a_pre, pre_tile_dep, None,
                   last_pe_tile, t_s0)
        fr = bld.frontier()
        t_gs = em("vector", lambda e: e.tensor_scalar(S_f[1][:], S_f[1][:], gate_c[:],
                                                      None, ALU.mult),
                  deps=fr + [t_gin], inc=sV)
        t_gsb = em("vector", lambda e: e.tensor_copy(S_bf[1][:], S_f[1][:]),
                   deps=[t_gs], inc=sV)

        def trunc_exit():
            frX = bld.frontier()
            lastt = None
            for o in range(16):
                lastt = em("sync", lambda e, o=o: e.dma_start(out_r[:, o, :], x1v(o)),
                           deps=frX, inc=sDS)
            em("sync", lambda e: e.nop(), deps=[lastt], inc=None)

        out_r = out_feat.rearrange("(o p) t -> o p t", p=128).transpose([1, 0, 2])
        if TRUNCATE == "preSSD":
            trunc_exit()
            with nc.Block() as block:
                bld.flush(block)
            return nc

        t_uown = load_ubf(u_own)
        main_tiles = {}
        halow = {}
        tap3_main = {}
        t_a_main_box = [None]

        def main_evac_deps(oc):
            d = []
            if oc < 32:
                d.append(last_pe_tile.get(oc))
                d += tap3_pre.get(oc, [])
                d.append(halo_toks[oc])
            return d

        def main_post(oc, t_e):
            if oc == 64:
                t_a_main_box[0] = softplus_stage(t_e, extra=[last_prol[0]])
                return
            if oc >= 48:
                return
            t = oc
            hdep = tap3_pre.get(t, []) + ([halo_toks[t]] if t < 32 else [t_haloc])
            halow[t] = em("vector", lambda e, t=t: e.tensor_scalar(
                xbc_tile(t, 0, 3), halo_sb[:, 3 * t:3 * t + 3], gate_c[:], None,
                ALU.mult),
                deps=[t_gin] + hdep, inc=sV)
            conv_tile(t, [t_e, halow[t]], tap3_main)

        evm = proj_stage(list(range(32)) + [64] + list(range(32, 48))
                         + list(range(48, 64)), t_uown,
                         z_to_silu=True, extra_evac_deps=main_evac_deps,
                         tile_tok=main_tiles, post_tile=main_post)
        t_a_main = t_a_main_box[0]
        if TRUNCATE == "postA":
            trunc_exit()
            with nc.Block() as block:
                bld.flush(block)
            return nc

        last_pe_tile2 = {}
        ssd_chunks(list(range(NCHUNK)), NCHUNK, False, t_a_main,
                   lambda t: list(tap3_main.get(t, [])),
                   lambda zt: evm[48 + zt], last_pe_tile2, t_gsb)

        if TRUNCATE == "postB":
            trunc_exit()
            with nc.Block() as block:
                bld.flush(block)
            return nc

        if DEBUG:
            dbg_xbc = nc.dram_tensor("dbg_xbc", [128, 48 * 1027], BF16,
                                     kind="ExternalOutput").ap()
            dbg_a = nc.dram_tensor("dbg_a", [32, 1024], F32, kind="ExternalOutput").ap()
            dbg_silu = nc.dram_tensor("dbg_silu", [128, 16 * 1024], BF16,
                                      kind="ExternalOutput").ap()
            dbg_yg = nc.dram_tensor("dbg_yg", [128, 16 * 1024], BF16,
                                    kind="ExternalOutput").ap()
            dbg_S = nc.dram_tensor("dbg_S", [128, 2048], F32, kind="ExternalOutput").ap()
            frD = bld.frontier()
            em("sync", lambda e: e.dma_start(dbg_xbc, xbc[:]), deps=frD, inc=sDS)
            em("sync", lambda e: e.dma_start(dbg_a, a_fm[:]), deps=frD, inc=sDS)
            em("sync", lambda e: e.dma_start(dbg_silu, silu_z[:]), deps=frD, inc=sDS)
            em("sync", lambda e: e.dma_start(dbg_yg, yg[:]), deps=frD, inc=sDS)
            em("sync", lambda e: e.dma_start(dbg_S[:, 0:1024], S_f[0][:]), deps=frD, inc=sDS)
            em("sync", lambda e: e.dma_start(dbg_S[:, 1024:2048], S_f[1][:]), deps=frD, inc=sDS)

        frB = bld.frontier()
        stat_banks = [ssdP[0], ssdP[1], ssdT[0], ssdT[1]]
        stat_bk = [bkP[0], bkP[1], bkT[0], bkT[1]]
        x1tok = {}
        statmm = {}

        def emit_stats(o, t_xb, t_sq, statmm_out):
            for tt in range(2):
                tmm1 = em("tensor", lambda e, o=o, tt=tt: e.matmul(
                    stat_banks[tt][0:1, 0:512], onescol_bf[:],
                    lnxb[o % 2][:, tt * 512:(tt + 1) * 512],
                    start=(o == 0), stop=(o == 15)),
                          deps=[t_xb, t_ocb] + (stat_bk[tt].wdeps() if o == 0 else []),
                          inc=sPE)
                tmm2 = em("tensor", lambda e, o=o, tt=tt: e.matmul(
                    stat_banks[2 + tt][0:1, 0:512], onescol_bf[:],
                    lnsq[o % 2][:, tt * 512:(tt + 1) * 512],
                    start=(o == 0), stop=(o == 15)),
                          deps=[t_sq] + (stat_bk[2 + tt].wdeps() if o == 0 else []),
                          inc=sPE)
                if o == 15:
                    statmm_out[tt] = (tmm1, tmm2)
                b_lnxb[o % 2].read(tmm1)
                b_lnsq[o % 2].read(tmm2)
                if o == 15:
                    stat_bk[tt].wrote(tmm1)
                    stat_bk[2 + tt].wrote(tmm2)

        pend_stats = [None]
        for o in range(16):
            buf, bbuf = woutsl[o % 2], b_wout[o % 2]
            t_l = em("gpsimd", lambda e, buf=buf, o=o: e.dma_start(
                buf[:].rearrange("p (kc o) -> p kc o", kc=NKC),
                slab_src(w_outT, o * 128, 128, NKC)),
                     deps=bbuf.wdeps() + (frB if o < 2 else []), inc=sDW)
            bbuf.wrote(t_l)
            t_u = em("sync", lambda e, o=o: e.dma_start(
                ures[o % 2][:], u_own[o * 128:(o + 1) * 128, :]),
                     deps=b_ures[o % 2].wdeps() + (frB if o < 2 else []), inc=sDS)
            b_ures[o % 2].wrote(t_u)
            pr, bk = pair_banks[o % 2]
            first_deps = [t_l] + bk.wdeps() + (frB if o < 2 else [])
            t_mm = None
            for kc in range(NKC):
                for tt in range(2):
                    last = (kc == NKC - 1 and tt == 1)

                    def mmfn(e, buf=buf, kc=kc, tt=tt, pr=pr):
                        return e.matmul(pr[:, tt * 512:(tt + 1) * 512],
                                        buf[:, kc * 128:(kc + 1) * 128],
                                        yg[:, kc * 1024 + tt * 512:kc * 1024 + (tt + 1) * 512],
                                        start=(kc == 0), stop=(kc == NKC - 1))
                    t_mm = em("tensor", mmfn, deps=(first_deps if (kc == 0 and tt == 0) else ()),
                              inc=(sPE if last else None))
            if pend_stats[0] is not None:
                emit_stats(*pend_stats[0])
                pend_stats[0] = None
            bk.wrote(t_mm)
            bbuf.read(t_mm)
            t_e = em("vector", lambda e, o=o, pr=pr: e.tensor_tensor(
                x1v(o), ures[o % 2][:], pr[:], ALU.add),
                     deps=[t_mm, t_u] + (frB if o < 2 else []), inc=sV)
            b_ures[o % 2].read(t_e)
            bk.read(t_e)
            x1tok[o] = t_e
            t_xb = em("vector", lambda e, o=o: e.tensor_copy(lnxb[o % 2][:], x1v(o)),
                      deps=[t_e] + b_lnxb[o % 2].wdeps(), inc=sV)
            b_lnxb[o % 2].wrote(t_xb)
            t_sq = em("scalar", lambda e, o=o: e.activation(lnsq[o % 2][:], x1v(o), AF.Square),
                      deps=[t_e] + b_lnsq[o % 2].wdeps(), inc=sSC)
            b_lnsq[o % 2].wrote(t_sq)
            pend_stats[0] = (o, t_xb, t_sq, statmm)
        emit_stats(*pend_stats[0])
        pend_stats[0] = None

        def ln_finalize_apply(x1dep, statmm, out_bf):
            toks = {}
            for tt in range(2):
                tsum, tsq = statmm[tt]
                tm = em("vector", lambda e, tt=tt: e.tensor_scalar(
                    ln_m[:], stat_banks[tt][0:1, 0:512], 1.0 / DM, None, ALU.mult),
                        deps=[tsum] + b_lnm.wdeps(), inc=sV)
                b_lnm.wrote(tm)
                stat_bk[tt].read(tm)
                tmsq = em("vector", lambda e, tt=tt: e.tensor_scalar(
                    ln_msq[:], stat_banks[2 + tt][0:1, 0:512], 1.0 / DM, None, ALU.mult),
                          deps=[tsq] + b_lnmsq.wdeps(), inc=sV)
                b_lnmsq.wrote(tmsq)
                stat_bk[2 + tt].read(tmsq)
                tm2 = em("scalar", lambda e: e.activation(ln_m2[:], ln_m[:], AF.Square),
                         deps=[tm] + b_lnm2.wdeps(), inc=sSC)
                b_lnm2.wrote(tm2)
                b_lnm.read(tm2)
                tv = em("vector", lambda e: e.tensor_tensor(ln_var[:], ln_msq[:], ln_m2[:],
                                                            ALU.subtract),
                        deps=[tmsq, tm2] + b_lnvar.wdeps(), inc=sV)
                b_lnvar.wrote(tv)
                b_lnmsq.read(tv)
                b_lnm2.read(tv)
                tl = em("scalar", lambda e: e.activation(ln_rstd[:], ln_var[:], AF.Ln,
                                                         bias=eps_c[:]),
                        deps=[tv, t_eps] + b_lnrstd.wdeps(), inc=sSC)
                tr = em("scalar", lambda e: e.activation(ln_rstd[:], ln_rstd[:], AF.Exp,
                                                         scale=-0.5),
                        deps=[tl], inc=sSC)
                b_lnrstd.wrote(tr)
                b_lnvar.read(tl)
                dA2 = bkA.wdeps()
                tbm = em("tensor", lambda e: e.matmul(pairA[:, 0:512], onesrow[:], ln_m[:],
                                                      start=True, stop=True),
                         deps=[tm, t_or] + dA2, inc=sPE)
                b_lnm.read(tbm)
                tbr = em("tensor", lambda e: e.matmul(pairA[:, 512:1024], onesrow[:], ln_rstd[:],
                                                      start=True, stop=True),
                         deps=[tr], inc=sPE)
                b_lnrstd.read(tbr)
                bkA.wrote(tbr)
                tmb = em("scalar", lambda e: e.copy(mb_sb[:], pairA[:, 0:512]),
                         deps=[tbm] + b_mb.wdeps(), inc=sSC)
                b_mb.wrote(tmb)
                bkA.read(tmb)
                for o in range(16):
                    t1 = em("vector", lambda e, o=o, tt=tt: e.tensor_tensor(
                        x1v(o)[:, tt * 512:(tt + 1) * 512],
                        x1v(o)[:, tt * 512:(tt + 1) * 512], mb_sb[:], ALU.subtract),
                            deps=[tmb, x1dep(o, tt)], inc=sV)
                    b_mb.read(t1)
                    t2 = em("vector", lambda e, o=o, tt=tt: e.tensor_tensor(
                        x1v(o)[:, tt * 512:(tt + 1) * 512],
                        x1v(o)[:, tt * 512:(tt + 1) * 512], pairA[:, 512:1024], ALU.mult),
                            deps=[t1, tbr], inc=sV)
                    bkA.read(t2)
                    tok = t2
                    if out_bf is not None:
                        tok = em("vector", lambda e, o=o, tt=tt: e.tensor_copy(
                            out_bf[:, o * 1024 + tt * 512:o * 1024 + (tt + 1) * 512],
                            x1v(o)[:, tt * 512:(tt + 1) * 512]),
                                 deps=[t2], inc=sV)
                    toks[(o, tt)] = tok
            return toks

        ln1 = ln_finalize_apply(lambda o, tt: x1tok[o], statmm, x1_bf)

        if DEBUG:
            dbg_x1 = nc.dram_tensor("dbg_x1", [128, 16 * 1024], F32,
                                    kind="ExternalOutput").ap()
            for o in range(16):
                em("sync", lambda e, o=o: e.dma_start(dbg_x1[:, o * 1024:(o + 1) * 1024],
                                                      x1v(o)),
                   deps=[ln1[(o, 0)], ln1[(o, 1)]], inc=sDS)

        if TRUNCATE == "postC":
            trunc_exit()
            with nc.Block() as block:
                bld.flush(block)
            return nc

        frC = bld.frontier()
        down_done = {}
        gate_banks = [(pairA, pairB, bkA, bkB),
                      (ssdP[0], ssdT[0], bkP[0], bkT[0])]
        lastln = [ln1[(15, 0)], ln1[(15, 1)]]
        for hb in range(2):
            gtoks = {}
            for m_ in range(32):
                m = hb * 32 + m_
                ex = frC if (hb == 0 and m_ < 2) else []
                bg, bu = b_wg[m_ % 2], b_wu[m_ % 2]
                t_lg = em("gpsimd", lambda e, m=m, m_=m_: e.dma_start(
                    wgsl[m_ % 2][:].rearrange("p (kc o) -> p kc o", kc=NKC),
                    slab_src(wgT, m * 128, 128, NKC)),
                          deps=bg.wdeps() + ex, inc=sDW)
                bg.wrote(t_lg)
                t_lu = em("gpsimd", lambda e, m=m, m_=m_: e.dma_start(
                    wusl[m_ % 2][:].rearrange("p (kc o) -> p kc o", kc=NKC),
                    slab_src(wuT, m * 128, 128, NKC)),
                          deps=bu.wdeps(), inc=sDW)
                bu.wrote(t_lu)
                if m_ % 2 == 0:
                    prg, pru, bkg, bku = pairA, pairB, bkA, bkB
                    gv = lambda tt: prg[:, tt * 512:(tt + 1) * 512]
                    uv = lambda tt: pru[:, tt * 512:(tt + 1) * 512]
                else:
                    gv = lambda tt: [ssdP[0], ssdP[1]][tt][:, 0:512]
                    uv = lambda tt: [ssdT[0], ssdT[1]][tt][:, 0:512]
                    bkg, bku = None, None
                if m_ % 2 == 0:
                    bdeps = bkA.wdeps() + bkB.wdeps()
                else:
                    bdeps = (bkP[0].wdeps() + bkP[1].wdeps()
                             + bkT[0].wdeps() + bkT[1].wdeps())
                first_deps = [t_lg, t_lu] + lastln + bdeps + ex
                t_mm = None
                for kc in range(NKC):
                    for tt in range(2):
                        for gu in range(2):
                            last = (kc == NKC - 1 and tt == 1 and gu == 1)

                            def mmfn(e, m_=m_, kc=kc, tt=tt, gu=gu, gv=gv, uv=uv):
                                w = (wgsl if gu == 0 else wusl)[m_ % 2]
                                pv = (gv if gu == 0 else uv)(tt)
                                return e.matmul(
                                    pv, w[:, kc * 128:(kc + 1) * 128],
                                    x1_bf[:, kc * 1024 + tt * 512:kc * 1024 + (tt + 1) * 512],
                                    start=(kc == 0), stop=(kc == NKC - 1))
                            t_mm = em("tensor", mmfn,
                                      deps=(first_deps if (kc == 0 and tt == 0 and gu == 0) else ()),
                                      inc=(sPE if last else None))
                bg.read(t_mm)
                bu.read(t_mm)
                gts = []
                for tt in range(2):
                    si = tt * 2 + m_ % 2
                    t_sg = em("scalar", lambda e, tt=tt, si=si, gv=gv: e.activation(
                        silu4[si][:], gv(tt), AF.Silu),
                              deps=[t_mm] + b_sil4[si].wdeps() + ex, inc=sSC)
                    b_sil4[si].wrote(t_sg)
                    t_gt = em("vector", lambda e, m_=m_, tt=tt, si=si, uv=uv: e.tensor_tensor(
                        gated[:, m_ * 1024 + tt * 512:m_ * 1024 + (tt + 1) * 512],
                        silu4[si][:], uv(tt), ALU.mult),
                              deps=[t_sg] + ex, inc=sV)
                    b_sil4[si].read(t_gt)
                    gts.append(t_gt)
                if m_ % 2 == 0:
                    bkA.wrote(t_mm); bkB.wrote(t_mm)
                    bkA.read(gts[0]); bkA.read(gts[1])
                    bkB.read(gts[0]); bkB.read(gts[1])
                else:
                    for bx, gt in ((bkP[0], gts[0]), (bkP[1], gts[1])):
                        bx.wrote(t_mm); bx.read(gt)
                    for bx in (bkT[0], bkT[1]):
                        bx.wrote(t_mm); bx.read(gts[0]); bx.read(gts[1])
                gtoks[m_] = gts
            glast = gtoks[31][1]
            for o in range(16):
                bw = b_wd[o % 2]
                t_ld = em("gpsimd", lambda e, o=o, hb=hb: e.dma_start(
                    wdsl[o % 2][:].rearrange("p (kc o) -> p kc o", kc=32),
                    wdT[hb * 4096:(hb + 1) * 4096, o * 128:(o + 1) * 128]
                    .rearrange("(kc p) o -> kc p o", p=128).transpose([1, 0, 2])),
                          deps=bw.wdeps() + (frC if (hb == 0 and o == 0) else []), inc=sDW)
                bw.wrote(t_ld)
                pr, bk = (pairA, bkA) if o % 2 == 0 else (pairB, bkB)
                t_mm = None
                for kc in range(32):
                    for tt in range(2):
                        last = (kc == 31 and tt == 1)
                        if kc == 0 and tt == 0:
                            deps0 = bk.wdeps() + [t_ld, glast]
                        else:
                            deps0 = ()

                        def mmfn(e, o=o, kc=kc, tt=tt, pr=pr):
                            return e.matmul(
                                pr[:, tt * 512:(tt + 1) * 512],
                                wdsl[o % 2][:, kc * 128:(kc + 1) * 128],
                                gated[:, kc * 1024 + tt * 512:kc * 1024 + (tt + 1) * 512],
                                start=(kc == 0), stop=(kc == 31))
                        t_mm = em("tensor", mmfn, deps=deps0, inc=(sPE if last else None))
                bk.wrote(t_mm)
                bw.read(t_mm)
                for tt in range(2):
                    t_add = em("vector", lambda e, o=o, tt=tt, pr=pr: e.tensor_tensor(
                        x1v(o)[:, tt * 512:(tt + 1) * 512],
                        x1v(o)[:, tt * 512:(tt + 1) * 512], pr[:, tt * 512:(tt + 1) * 512],
                        ALU.add),
                               deps=[t_mm, ln1[(o, tt)]]
                               + ([down_done[(o, tt)]] if hb == 1 else []), inc=sV)
                    bk.read(t_add)
                    down_done[(o, tt)] = t_add

        frD2 = bld.frontier()
        statmm2 = {}
        for o in range(16):
            dd = [down_done[(o, 0)], down_done[(o, 1)]] + (frD2 if o == 0 else [])
            t_xb = em("vector", lambda e, o=o: e.tensor_copy(lnxb[o % 2][:], x1v(o)),
                      deps=dd + b_lnxb[o % 2].wdeps(), inc=sV)
            b_lnxb[o % 2].wrote(t_xb)
            t_sq = em("scalar", lambda e, o=o: e.activation(lnsq[o % 2][:], x1v(o), AF.Square),
                      deps=dd + b_lnsq[o % 2].wdeps(), inc=sSC)
            b_lnsq[o % 2].wrote(t_sq)
            if pend_stats[0] is not None:
                emit_stats(*pend_stats[0])
                pend_stats[0] = None
            pend_stats[0] = (o, t_xb, t_sq, statmm2)
        emit_stats(*pend_stats[0])
        pend_stats[0] = None
        if DEBUG:
            dbg_st = nc.dram_tensor("dbg_st", [4, 512], F32, kind="ExternalOutput").ap()
            stt = []
            for i in range(4):
                tt_ = i % 2
                src = stat_banks[i // 2 * 2 + tt_] if False else stat_banks[[0, 1, 2, 3][i]]
                tcp = em("vector", lambda e, i=i: e.tensor_copy(
                    mb_sb[0:1, 0:512], stat_banks[i][0:1, 0:512]),
                         deps=[statmm2[0][0], statmm2[0][1], statmm2[1][0], statmm2[1][1]]
                         + b_mb.wdeps(), inc=sV)
                b_mb.wrote(tcp)
                td = em("sync", lambda e, i=i: e.dma_start(dbg_st[i:i + 1, :], mb_sb[0:1, 0:512]),
                        deps=[tcp], inc=sDS)
                b_mb.read(td)
                stt.append(td)
        ln2 = ln_finalize_apply(lambda o, tt: down_done[(o, tt)], statmm2, None)

        last = None
        for o in range(16):
            last = em("sync", lambda e, o=o: e.dma_start(out_r[:, o, :], x1v(o)),
                      deps=[ln2[(o, 0)], ln2[(o, 1)]], inc=sDS)
        em("sync", lambda e: e.nop(), deps=[last], inc=None)

        with nc.Block() as block:
            bld.flush(block)

    return nc


_NC_CACHE = None


def kernel(L0_attn_u, L0_attn_W_in, L0_attn_conv_w, L0_attn_conv_b, L0_attn_z_bias,
           L0_attn_D, L0_attn_W_out, L0_ffn_w_gate, L0_ffn_w_up, L0_ffn_w_down):
    global _NC_CACHE
    u = np.asarray(L0_attn_u, np.float32)
    w_inT = np.ascontiguousarray(np.asarray(L0_attn_W_in, np.float32).T)
    w_outT = np.ascontiguousarray(np.asarray(L0_attn_W_out, np.float32).T)
    wgT = np.ascontiguousarray(np.asarray(L0_ffn_w_gate, np.float32).T)
    wuT = np.ascontiguousarray(np.asarray(L0_ffn_w_up, np.float32).T)
    wdT = np.ascontiguousarray(np.asarray(L0_ffn_w_down, np.float32).T)
    convw = np.ascontiguousarray(
        np.asarray(L0_attn_conv_w, np.float32).reshape(48, 128, 4).transpose(1, 0, 2)
        .reshape(128, 192))
    convb = np.ascontiguousarray(np.asarray(L0_attn_conv_b, np.float32).reshape(48, 128).T)
    zb = np.ascontiguousarray(np.asarray(L0_attn_z_bias, np.float32).reshape(16, 128).T)
    drow = np.ascontiguousarray(np.asarray(L0_attn_D, np.float32).reshape(1, NH))

    identf = np.eye(128, dtype=np.float32)
    identbf = np.eye(128).astype(bf16)
    triu = np.triu(np.ones((128, 128), np.float32))
    ii = np.arange(128)
    negmaskT = np.where(ii[:, None] < ii[None, :], BIG, 0.0).astype(np.float32).astype(bf16)
    onesrow = np.ones((1, 128), np.float32)

    u_feat = np.ascontiguousarray(u.transpose(0, 2, 1))

    if _NC_CACHE is None:
        _NC_CACHE = build_nc()
    nc = _NC_CACHE

    in_maps = []
    for c in range(8):
        b, half = c // 2, c % 2
        own = np.ascontiguousarray(u_feat[b][:, half * 1024:(half + 1) * 1024])
        if half == 1:
            pre = np.ascontiguousarray(u_feat[b][:, 0:1024])
            gate = np.ones((128, 1), np.float32)
        else:
            pre = np.zeros((DM, TOWN), np.float32)
            gate = np.zeros((128, 1), np.float32)
        in_maps.append({
            "u_own": own, "u_pre": pre, "gate": gate,
            "w_inT": w_inT, "w_outT": w_outT, "wgT": wgT, "wuT": wuT, "wdT": wdT,
            "convw": convw, "convb": convb, "zb": zb, "drow": drow,
            "identf": identf, "identbf": identbf, "triu": triu,
            "negmaskT": negmaskT, "onesrow": onesrow,
        })

    res = run_bass_kernel_spmd(nc, in_maps, core_ids=list(range(8)))
    global _LAST_RES
    _LAST_RES = res
    out = np.empty((4, 2048, 2048), np.float32)
    for c in range(8):
        b, half = c // 2, c % 2
        out[b, half * 1024:(half + 1) * 1024, :] = res.results[c]["out_feat"].T
    return out


# version 06
# speedup vs baseline: 1.2291x; 1.0871x over previous
import numpy as np
import ml_dtypes
from contextlib import ExitStack

import concourse.bass as bass
import concourse.mybir as mybir
from concourse.bass_utils import run_bass_kernel_spmd

F32 = mybir.dt.float32
BF16 = mybir.dt.bfloat16
U8 = mybir.dt.uint8
AF = mybir.ActivationFunctionType
ALU = mybir.AluOpType
bf16 = ml_dtypes.bfloat16

DM = 2048
NH = 32
TOWN = 1024
Q = 128
NCHUNK = TOWN // Q
NKC = DM // 128
BIG = 3.0e38
EPS = 1e-5

ENGS = ("tensor", "vector", "scalar", "gpsimd", "sync")
DEBUG = False
TRUNCATE = None
SSD_MODE = "full"
_LAST_RES = None


class Stream:
    __slots__ = ("h", "n", "step", "name")

    def __init__(self, h, name, step):
        self.h = h
        self.n = 0
        self.step = step
        self.name = name


class Builder:
    def __init__(self, nc, stack):
        self.nc = nc
        self.ops = {e: [] for e in ENGS}
        self.streams = []
        self.stack = stack

    def stream(self, name, step=1):
        h = self.stack.enter_context(self.nc.semaphore(name))
        s = Stream(h, name, step)
        self.streams.append(s)
        return s

    def emit(self, eng, fn, deps=(), inc=None):
        tok = None
        if inc is not None:
            inc.n += inc.step
            tok = (inc, inc.n)
        dd = tuple(t for t in deps if t is not None)
        self.ops[eng].append((fn, dd, inc, inc.step if inc else 0))
        return tok

    def frontier(self):
        return [(s, s.n) for s in self.streams if s.n > 0]

    def flush(self, block):
        for e in ENGS:
            ops = self.ops[e]

            def make(ops):
                def f(eng):
                    lastw = {}
                    for fn, deps, st, stp in ops:
                        for (ds, dv) in deps:
                            if dv > 0 and lastw.get(ds.name, 0) < dv:
                                eng.wait_ge(ds.h, dv)
                                lastw[ds.name] = dv
                        ins = fn(eng)
                        if st is not None:
                            ins.then_inc(st.h, stp)
                return f

            getattr(block, e)(make(ops))


class Bank:
    def __init__(self):
        self.wtok = None
        self.rtoks = []

    def wdeps(self):
        d = list(self.rtoks)
        if self.wtok is not None:
            d.append(self.wtok)
        self.rtoks = []
        return d

    def wrote(self, tok):
        self.wtok = tok

    def rdeps(self):
        return [self.wtok] if self.wtok is not None else []

    def read(self, tok):
        if tok is not None:
            self.rtoks.append(tok)


def build_nc():
    nc = bass.Bass("TRN2", target_bir_lowering=False)

    def dt_in(name, shape, dt=F32):
        return nc.dram_tensor(name, list(shape), dt, kind="ExternalInput").ap()

    u_own = dt_in("u_own", (DM, TOWN))
    u_pre = dt_in("u_pre", (DM, TOWN))
    w_inT = dt_in("w_inT", (DM, 8224))
    w_outT = dt_in("w_outT", (2048, DM))
    wgT = dt_in("wgT", (DM, 8192))
    wuT = dt_in("wuT", (DM, 8192))
    wdT = dt_in("wdT", (8192, DM))
    convw_in = dt_in("convw", (128, 192))
    convb_in = dt_in("convb", (128, 48))
    zb_in = dt_in("zb", (128, 16))
    drow_in = dt_in("drow", (1, NH))
    gate_in = dt_in("gate", (128, 1))
    identf_in = dt_in("identf", (128, 128))
    identbf_in = dt_in("identbf", (128, 128), BF16)
    triu_in = dt_in("triu", (128, 128))
    negmask_in = dt_in("negmaskT", (128, 128), BF16)
    onesrow_in = dt_in("onesrow", (1, 128))
    out_feat = nc.dram_tensor("out_feat", [DM, TOWN], F32, kind="ExternalOutput").ap()

    stack = ExitStack()
    with stack:
        LIMIT = 212608
        arena = stack.enter_context(nc.sbuf_tensor("arena", [128, LIMIT], U8))

        def carve(off, rows, cols, dt, p0=0):
            nb = cols * mybir.dt.size(dt)
            assert off % 4 == 0 and off + nb <= LIMIT, (off, nb)
            return arena[p0:p0 + rows, off:off + nb].bitcast(dt)

        identf = carve(0, 128, 128, F32)
        triu = carve(512, 128, 128, F32)
        identbf = carve(1024, 128, 128, BF16)
        negmask = carve(1280, 128, 128, BF16)
        convw = carve(1536, 128, 192, F32)
        convb = carve(2304, 128, 48, F32)
        zbias = carve(2496, 128, 16, F32)
        d_bc = carve(2560, 128, 32, F32)
        gate_c = carve(2688, 128, 1, F32)
        onescol_bf = carve(2692, 128, 1, BF16)
        eps_c = carve(2696, 1, 1, F32)
        onesrow = carve(2752, 1, 128, F32)
        drow = carve(3264, 1, 32, F32)
        S_f = [carve(3392, 128, 1024, F32), carve(7488, 128, 1024, F32)]
        S_bf = [carve(11584, 128, 1024, BF16), carve(13632, 128, 1024, BF16)]
        x1_o15 = carve(3392, 128, 1024, F32)
        woutsl = [carve(7488, 128, 2048, BF16), carve(11584, 128, 2048, BF16)]
        silu4 = [carve(7488 + 1024 * i, 128, 512, BF16) for i in range(4)]
        XBC0 = 15680
        xbc = carve(XBC0, 128, 48 * 1027, BF16)

        def xbc_tile(t, c0, c1):
            return xbc[:, t * 1027 + c0: t * 1027 + c1]

        gated = carve(XBC0, 128, 64 * 512, BF16)
        x1_bf = carve(XBC0 + 65536, 128, 16 * 1024, BF16)
        lnxb = [carve(XBC0, 128, 1024, BF16), carve(XBC0 + 2048, 128, 1024, BF16)]
        lnsq = [carve(XBC0 + 4096, 128, 1024, BF16), carve(XBC0 + 6144, 128, 1024, BF16)]
        ln_m = carve(XBC0 + 8192, 1, 512, F32)
        ln_msq = carve(XBC0 + 10240, 1, 512, F32)
        ln_var = carve(XBC0 + 12288, 1, 512, F32)
        ln_rstd = carve(XBC0 + 14336, 1, 512, F32)
        ln_m2 = carve(XBC0 + 16384, 1, 512, F32)
        mb_sb = carve(XBC0 + 18432, 128, 512, F32)
        ures = [carve(XBC0 + 20480, 128, 1024, F32), carve(XBC0 + 24576, 128, 1024, F32)]
        UBF0 = 114272
        u_bf = carve(UBF0, 128, 16 * 1024, BF16)
        yg = carve(UBF0, 128, 16 * 1024, BF16)
        wgsl = [carve(UBF0, 128, 2048, BF16), carve(UBF0 + 4096, 128, 2048, BF16)]
        wusl = [carve(UBF0 + 8192, 128, 2048, BF16), carve(UBF0 + 12288, 128, 2048, BF16)]
        wdsl = [carve(UBF0 + 16384, 128, 4096, BF16), carve(UBF0 + 24576, 128, 4096, BF16)]
        SIL0 = 147040
        silu_z = carve(SIL0, 128, 16 * 1024, BF16)
        x1_lo = carve(SIL0, 128, 8 * 1024, F32)
        TOP0 = 179808
        winsl = [carve(TOP0, 128, 2048, BF16), carve(TOP0 + 4096, 128, 2048, BF16)]
        SS0 = TOP0 + 8192
        cacc = [carve(SS0, 128, 512, F32), carve(SS0 + 2048, 128, 512, F32)]
        a_tm_sb = carve(SS0 + 4096, 128, 32, F32)
        negp_tm = carve(SS0 + 4224, 128, 32, F32)
        negp_rows2 = carve(SS0 + 4352, 32, 256, F32)
        wstate = carve(SS0 + 5376, 128, 32, F32)
        lamb = carve(SS0 + 5504, 128, 32, F32)
        wslog = carve(SS0 + 5632, 128, 32, F32)
        L_sb = [carve(SS0 + 5760, 128, 128, F32), carve(SS0 + 6272, 128, 128, F32)]
        M_sb4 = [carve(SS0 + 6784 + 256 * i, 128, 128, BF16) for i in range(4)]
        wb_sb = [carve(SS0 + 7808, 128, 128, F32), carve(SS0 + 8320, 128, 128, F32)]
        ct_sb4 = [carve(SS0 + 8832 + 256 * i, 128, 128, BF16) for i in range(4)]
        xt_sb = [carve(SS0 + 9856, 128, 128, BF16), carve(SS0 + 10112, 128, 128, BF16)]
        bt_sb = [carve(SS0 + 10368, 128, 64, BF16), carve(SS0 + 10496, 128, 64, BF16)]
        yr_sb = [carve(SS0 + 10624, 128, 128, F32), carve(SS0 + 11136, 128, 128, F32)]
        halo_sb = carve(SS0 + 11648, 128, 144, BF16)
        A_fm = carve(SS0 + 11936, 32, 1024, F32)
        sp_t2 = carve(SS0 + 16032, 32, 1024, F32)
        a_fm = carve(SS0 + 20128, 32, 1024, F32)
        assert SS0 + 24224 <= LIMIT
        x1_mid = carve(TOP0, 128, 7 * 1024, F32)

        def x1v(o):
            if o < 8:
                return x1_lo[:, o * 1024:(o + 1) * 1024]
            if o < 15:
                return x1_mid[:, (o - 8) * 1024:(o - 7) * 1024]
            return x1_o15

        pairA = stack.enter_context(nc.psum_tensor("pairA", [128, 1024], F32))
        pairB = stack.enter_context(nc.psum_tensor("pairB", [128, 1024], F32))
        ssdP = [stack.enter_context(nc.psum_tensor(f"ssdP{i}", [128, 512], F32)) for i in range(2)]
        ssdT = [stack.enter_context(nc.psum_tensor(f"ssdT{i}", [128, 512], F32)) for i in range(2)]

        bld = Builder(nc, stack)
        sDW = bld.stream("sDW", 16)
        sDS = bld.stream("sDS", 16)
        sPE = bld.stream("sPE", 1)
        sV = bld.stream("sV", 1)
        sSC = bld.stream("sSC", 1)
        sGP = bld.stream("sGP", 1)
        em = bld.emit

        bkA, bkB = Bank(), Bank()
        bkP = [Bank(), Bank()]
        bkT = [Bank(), Bank()]
        b_winsl = [Bank(), Bank()]
        b_wout = [Bank(), Bank()]
        b_wg = [Bank(), Bank()]
        b_wu = [Bank(), Bank()]
        b_wd = [Bank(), Bank()]
        b_cacc = [Bank(), Bank()]
        b_ures = [Bank(), Bank()]
        b_lnxb = [Bank(), Bank()]
        b_lnsq = [Bank(), Bank()]
        b_sil4 = [Bank() for _ in range(4)]
        b_Ls = [Bank(), Bank()]
        b_Ms4 = [Bank() for _ in range(4)]
        b_wb = [Bank(), Bank()]
        b_ct4 = [Bank() for _ in range(4)]
        M_tok = [None] * 4
        C_tok = [None] * 4
        b_xt = [Bank(), Bank()]
        b_bt = [Bank(), Bank()]
        b_yr = [Bank(), Bank()]
        b_ubf = Bank()
        b_atm = Bank()
        b_negptm = Bank()
        b_rows2 = Bank()
        b_wstate = Bank()
        b_lamb = Bank()
        b_lnm = Bank()
        b_lnmsq = Bank()
        b_lnvar = Bank()
        b_lnm2 = Bank()
        b_lnrstd = Bank()
        b_mb = Bank()

        t_id = em("sync", lambda e: e.dma_start(identf[:], identf_in), inc=sDS)
        t_idb = em("sync", lambda e: e.dma_start(identbf[:], identbf_in), inc=sDS)
        t_tri = em("sync", lambda e: e.dma_start(triu[:], triu_in), inc=sDS)
        t_nm = em("sync", lambda e: e.dma_start(negmask[:], negmask_in), inc=sDS)
        t_cw = em("sync", lambda e: e.dma_start(convw[:], convw_in), inc=sDS)
        t_cb = em("sync", lambda e: e.dma_start(convb[:], convb_in), inc=sDS)
        t_zb = em("sync", lambda e: e.dma_start(zbias[:], zb_in), inc=sDS)
        t_dr = em("sync", lambda e: e.dma_start(drow[:], drow_in), inc=sDS)
        t_gin = em("sync", lambda e: e.dma_start(gate_c[:], gate_in), inc=sDS)
        t_or = em("sync", lambda e: e.dma_start(onesrow[:], onesrow_in), inc=sDS)
        t_ocb = em("vector", lambda e: e.memset(onescol_bf[:], 1.0), inc=sV)
        t_eps = em("vector", lambda e: e.memset(eps_c[:], EPS), inc=sV)
        t_s0 = em("vector", lambda e: e.memset(S_f[1][:], 0.0), inc=sV)
        hz = {}
        for t in range(32):
            hz[t] = em("vector", lambda e, t=t: e.memset(xbc_tile(t, 0, 3), 0.0), inc=sV)
        t_dmm = em("tensor", lambda e: e.matmul(pairA[:, 0:32], onesrow[:], drow[:],
                                                start=True, stop=True),
                   deps=[t_or, t_dr], inc=sPE)
        t_dbc = em("vector", lambda e: e.tensor_copy(d_bc[:], pairA[:, 0:32]),
                   deps=[t_dmm], inc=sV)
        bkA.wrote(t_dmm)
        bkA.read(t_dbc)

        def load_ubf(src):
            tok = em("gpsimd",
                     lambda e: e.dma_start(
                         u_bf[:].rearrange("p (kc t) -> p kc t", kc=NKC),
                         src.rearrange("(kc p) t -> kc p t", p=128).transpose([1, 0, 2])),
                     deps=b_ubf.wdeps(), inc=sDW)
            b_ubf.wrote(tok)
            return tok

        def slab_src(wT, o0, ow, nkc):
            return wT[:, o0:o0 + ow].rearrange("(kc p) o -> kc p o", p=128).transpose([1, 0, 2])

        pair_banks = [(pairA, bkA), (pairB, bkB)]

        def proj_stage(oclist, u_ready, z_to_silu, extra_evac_deps, tile_tok,
                       post_tile=None):
            evtok = {}
            for gi, oc in enumerate(oclist):
                ow = 32 if oc == 64 else 128
                buf, bbuf = winsl[gi % 2], b_winsl[gi % 2]
                t_l = em("gpsimd",
                         lambda e, buf=buf, oc=oc, ow=ow: e.dma_start(
                             buf[:, 0:NKC * ow].rearrange("p (kc o) -> p kc o", kc=NKC),
                             slab_src(w_inT, oc * 128, ow, NKC)),
                         deps=bbuf.wdeps(), inc=sDW)
                bbuf.wrote(t_l)
                pr, bk = pair_banks[gi % 2]
                first_deps = [t_l, u_ready] + bk.wdeps()
                t_mm = None
                for kc in range(NKC):
                    for tt in range(2):
                        last = (kc == NKC - 1 and tt == 1)

                        def mmfn(e, buf=buf, kc=kc, tt=tt, ow=ow, pr=pr):
                            return e.matmul(pr[0:ow, tt * 512:(tt + 1) * 512],
                                            buf[:, kc * ow:(kc + 1) * ow],
                                            u_bf[:, kc * 1024 + tt * 512:kc * 1024 + (tt + 1) * 512],
                                            start=(kc == 0), stop=(kc == NKC - 1))
                        t_mm = em("tensor", mmfn,
                                  deps=(first_deps if (kc == 0 and tt == 0) else ()),
                                  inc=(sPE if last else None))
                bk.wrote(t_mm)
                bbuf.read(t_mm)
                b_ubf.read(t_mm)
                ed = [t_mm] + [d for d in extra_evac_deps(oc) if d is not None]
                if oc == 64:
                    t_e = em("vector", lambda e, pr=pr: e.tensor_copy(A_fm[:], pr[0:32, :]),
                             deps=ed, inc=sV)
                elif oc >= 48 and z_to_silu:
                    j = oc - 48
                    t_e = em("scalar",
                             lambda e, pr=pr, j=j: e.activation(
                                 silu_z[:, j * 1024:(j + 1) * 1024], pr[:], AF.Silu,
                                 bias=zbias[:, j:j + 1], scale=1.0),
                             deps=ed + [t_zb], inc=sSC)
                else:
                    t = oc
                    if gi % 3 != 2:
                        t_e = em("vector", lambda e, pr=pr, t=t: e.tensor_copy(
                            xbc_tile(t, 3, 1027), pr[:]), deps=ed, inc=sV)
                    else:
                        t_e = em("scalar", lambda e, pr=pr, t=t: e.copy(
                            xbc_tile(t, 3, 1027), pr[:]), deps=ed, inc=sSC)
                    if tile_tok is not None:
                        tile_tok[t] = t_e
                bk.read(t_e)
                evtok[oc] = t_e
                if post_tile is not None:
                    post_tile(oc, t_e)
            return evtok

        conv_n = [0]

        def conv_tile(t, deps0, tap3):
                for hh in (1, 0):
                    c0 = hh * 512
                    buf, bb = cacc[conv_n[0] % 2], b_cacc[conv_n[0] % 2]
                    conv_n[0] += 1
                    d0 = list(deps0) + bb.wdeps() + [t_cw, t_cb]
                    t0 = em("scalar",
                            lambda e, t=t, c0=c0, buf=buf: e.activation(
                                buf[:], xbc_tile(t, c0, c0 + 512), AF.Identity,
                                bias=convb[:, t:t + 1], scale=convw[:, 4 * t:4 * t + 1]),
                            deps=d0, inc=sSC)
                    tprev = t0
                    for k in (1, 2):
                        tprev = em("vector",
                                   lambda e, t=t, c0=c0, k=k, buf=buf: e.scalar_tensor_tensor(
                                       buf[:], xbc_tile(t, c0 + k, c0 + k + 512),
                                       convw[:, 4 * t + k:4 * t + k + 1], buf[:],
                                       ALU.mult, ALU.add),
                                   deps=[tprev], inc=sV)
                    t3 = em("vector",
                            lambda e, t=t, c0=c0, buf=buf: e.scalar_tensor_tensor(
                                xbc_tile(t, c0 + 3, c0 + 515),
                                xbc_tile(t, c0 + 3, c0 + 515),
                                convw[:, 4 * t + 3:4 * t + 4], buf[:],
                                ALU.mult, ALU.add),
                            deps=[tprev], inc=sV)
                    bb.wrote(t3)
                    bb.read(t3)
                    tap3.setdefault(t, []).append(t3)

        def softplus_stage(dep, extra=()):
            t1 = em("vector", lambda e: e.tensor_scalar(a_fm[:], A_fm[:], -1.0, None, ALU.mult),
                    deps=[dep] + list(extra), inc=sV)
            t2 = em("vector", lambda e: e.tensor_tensor(a_fm[:], A_fm[:], a_fm[:], ALU.min),
                    deps=[t1], inc=sV)
            t3 = em("scalar", lambda e: e.activation(a_fm[:], a_fm[:], AF.Exp),
                    deps=[t2], inc=sSC)
            t4 = em("scalar", lambda e: e.activation(a_fm[:], a_fm[:], AF.Ln, bias=1.0),
                    deps=[t3], inc=sSC)
            t5 = em("vector", lambda e: e.tensor_scalar(sp_t2[:], A_fm[:], 0.0, None, ALU.max),
                    deps=[dep], inc=sV)
            t6 = em("vector", lambda e: e.scalar_tensor_tensor(
                a_fm[:], a_fm[:], -1.0, sp_t2[:], ALU.mult, ALU.subtract),
                    deps=[t4, t5], inc=sV)
            return t6

        sbf_tok = {}
        last_prol = [None]

        class Slot:
            __slots__ = ("ap", "base", "bank")

            def __init__(self, ap, base, bank):
                self.ap, self.base, self.bank = ap, base, bank

            def v(self, lo, hi):
                return self.ap[:, self.base + lo:self.base + hi]

            def vp(self, p0, p1, lo, hi):
                return self.ap[p0:p1, self.base + lo:self.base + hi]

        bkA2, bkA3, bkB2, bkB3 = Bank(), Bank(), Bank(), Bank()
        Tslots = [Slot(ssdT[0], 0, bkT[0]), Slot(ssdT[1], 0, bkT[1]),
                  Slot(pairA, 0, bkA2), Slot(pairA, 512, bkA3)]
        Lslots = [Slot(ssdP[0], 0, bkP[0]), Slot(ssdP[1], 0, bkP[1]),
                  Slot(pairB, 0, bkB2), Slot(pairB, 512, bkB3)]

        def seed_slots():
            for t in bkA.wdeps():
                bkA2.read(t)
                bkA3.read(t)
            for t in bkB.wdeps():
                bkB2.read(t)
                bkB3.read(t)

        def merge_slots():
            for sb, dst in ((bkA2, bkA), (bkA3, bkA), (bkB2, bkB), (bkB3, bkB)):
                if sb.wtok is not None:
                    dst.read(sb.wtok)
                for t in sb.rtoks:
                    dst.read(t)

        def ssd_prologue(c, a_dep):
            p0 = ssdP[0]
            d0 = bkP[0].wdeps() + [a_dep, t_id] + b_atm.wdeps()
            tk1 = em("tensor", lambda e, c=c: e.transpose(
                p0[0:128, 0:32], a_fm[:, c * Q:(c + 1) * Q], identf[0:32, 0:32]),
                     deps=d0, inc=sPE)
            last_prol[0] = tk1
            te1 = em("vector", lambda e: e.tensor_copy(a_tm_sb[:], p0[0:128, 0:32]),
                     deps=[tk1], inc=sV)
            b_atm.wrote(te1)
            tk2 = em("tensor", lambda e: e.matmul(p0[:, 32:64], triu[:], a_tm_sb[:],
                                                  start=True, stop=True),
                     deps=[te1, t_tri], inc=sPE)
            b_atm.read(tk2)
            te2 = em("vector", lambda e: e.tensor_scalar(negp_tm[:], p0[:, 32:64],
                                                         -1.0, None, ALU.mult),
                     deps=[tk2] + b_negptm.wdeps(), inc=sV)
            b_negptm.wrote(te2)
            tk3 = em("tensor", lambda e: e.transpose(p0[0:32, 64:192], negp_tm[:], identf[:]),
                     deps=[te2], inc=sPE)
            b_negptm.read(tk3)
            d3 = b_rows2.wdeps()
            te3a = em("vector", lambda e: e.tensor_copy(negp_rows2[:, 0:128], p0[0:32, 64:192]),
                      deps=[tk3] + d3, inc=sV)
            te3b = em("vector", lambda e: e.tensor_copy(negp_rows2[:, 128:256], p0[0:32, 64:192]),
                      deps=[tk3], inc=sV)
            b_rows2.wrote(te3b)
            tk4 = em("tensor", lambda e: e.matmul(
                p0[:, 192:224], negp_rows2[:, 127:128].broadcast_to([32, 128]),
                identf[0:32, 0:32], start=True, stop=True),
                     deps=[te3a, te3b], inc=sPE)
            b_rows2.read(tk4)
            tw1 = em("vector", lambda e: e.tensor_tensor(wslog[:], negp_tm[:], p0[:, 192:224],
                                                         ALU.subtract),
                     deps=[tk4], inc=sV)
            b_negptm.read(tw1)
            dw = b_wstate.wdeps()
            tw2 = em("scalar", lambda e: e.activation(wstate[:], wslog[:], AF.Exp),
                     deps=[tw1] + dw, inc=sSC)
            b_wstate.wrote(tw2)
            dl = b_lamb.wdeps()
            tw3 = em("scalar", lambda e: e.activation(lamb[:], p0[:, 192:224], AF.Exp, scale=-1.0),
                     deps=[tk4, tw1] + dl, inc=sSC)
            b_lamb.wrote(tw3)
            bkP[0].wrote(tk4)
            bkP[0].read(te1)
            bkP[0].read(te2)
            bkP[0].read(te3b)
            bkP[0].read(tw1)
            bkP[0].read(tw3)
            return te3b, te2, tw2, tw3

        def emit_pair_back(ctx):
            (c, pw, j, sl, tev_x, tsn_toks, z_dep, state_dep) = ctx
            par = j % 2
            bt = sl.bank
            for hh in range(2):
                h = 2 * j + hh
                bb = 64 * hh
                mi = hh * 2 + (j % 2)
                tM = M_tok[mi]
                tC = C_tok[mi]
                sdep = sbf_tok.get((1 - pw, h), state_dep)
                em("tensor", lambda e, hh=hh, bb=bb, par=par, mi=mi, sl=sl: e.matmul(
                    sl.vp(bb, bb + 64, 128, 256), xt_sb[par][:, bb:bb + 64], M_sb4[mi][:],
                    start=True, stop=False, tile_position=(0, bb)),
                   deps=[tM, tev_x] + tsn_toks, inc=None)
                tY2 = em("tensor", lambda e, hh=hh, bb=bb, pw=pw, h=h, mi=mi, sl=sl: e.matmul(
                    sl.vp(bb, bb + 64, 128, 256),
                    S_bf[1 - pw][bb:bb + 64, (h // 2) * 64:(h // 2) * 64 + 64],
                    ct_sb4[mi][bb:bb + 64, :],
                    start=False, stop=True, tile_position=(bb, bb)),
                         deps=[tC, sdep], inc=sPE)
                b_Ms4[mi].read(tY2)
                b_ct4[mi].read(tY2)
                b_xt[par].read(tY2)
                tYr = em("vector", lambda e, j=j, c=c, hh=hh, bb=bb, h=h, sl=sl:
                         e.scalar_tensor_tensor(
                             yr_sb[hh][bb:bb + 64, :],
                             xbc_tile(j, 3 + c * Q, 3 + (c + 1) * Q)[bb:bb + 64, :],
                             d_bc[bb:bb + 64, h:h + 1],
                             sl.vp(bb, bb + 64, 128, 256), ALU.mult, ALU.add),
                         deps=[tY2, t_dbc] + b_yr[hh].wdeps(), inc=sV)
                b_yr[hh].wrote(tYr)
                bt.read(tYr)
                tyg = em("vector", lambda e, c=c, hh=hh, bb=bb, h=h: e.tensor_tensor(
                    yg[bb:bb + 64, (h // 2) * 1024 + c * Q:(h // 2) * 1024 + (c + 1) * Q],
                    yr_sb[hh][bb:bb + 64, :],
                    silu_z[bb:bb + 64, (h // 2) * 1024 + c * Q:(h // 2) * 1024 + (c + 1) * Q],
                    ALU.mult),
                          deps=[tYr, z_dep(h // 2)], inc=sV)
                b_yr[hh].read(tyg)

        def ssd_chunks(chunks, glb_off, state_only, a_dep, tile_dep, z_dep,
                       last_pe_tile, state_dep):
            for ci, c in enumerate(chunks):
                cg = glb_off + ci
                pw = cg % 2
                te_rows, te_np, tw_ws, tw_lam = ssd_prologue(c, a_dep)
                prev = None
                for j in range(16):
                    par = j % 2
                    sl = Tslots[j % 4]
                    bt = sl.bank
                    trdeps = bt.wdeps() + list(tile_dep(16 + j)) + list(tile_dep(j)) + [t_idb]
                    tb = em("tensor", lambda e, j=j, c=c, sl=sl: e.transpose(
                        sl.v(0, 64).bitcast(BF16),
                        xbc_tile(16 + j, 3 + c * Q, 3 + (c + 1) * Q), identbf[:]),
                            deps=trdeps, inc=sPE)
                    tx = em("tensor", lambda e, j=j, c=c, sl=sl: e.transpose(
                        sl.v(64, 128).bitcast(BF16),
                        xbc_tile(j, 3 + c * Q, 3 + (c + 1) * Q), identbf[:]),
                            deps=[], inc=sPE)
                    last_pe_tile[16 + j] = tb
                    last_pe_tile[j] = tx
                    tev_x = em("scalar", lambda e, par=par, sl=sl: e.copy(
                        xt_sb[par][:], sl.v(64, 128).bitcast(BF16)),
                               deps=[tx] + b_xt[par].wdeps(), inc=sSC)
                    b_xt[par].wrote(tev_x)
                    bts = []
                    for hh in range(2):
                        h = 2 * j + hh
                        tbh = em("scalar", lambda e, hh=hh, h=h, sl=sl: e.activation(
                            bt_sb[hh][:],
                            sl.v(0, 64).bitcast(BF16)[:, hh * 64:(hh + 1) * 64],
                            AF.Identity, scale=wstate[:, h:h + 1]),
                                 deps=[tb, tw_ws] + b_bt[hh].wdeps(), inc=sSC)
                        b_bt[hh].wrote(tbh)
                        b_wstate.read(tbh)
                        bts.append(tbh)
                    bt.wrote(tx)
                    bt.read(tev_x)
                    bt.read(bts[0])
                    bt.read(bts[1])
                    if not state_only and SSD_MODE != "state":
                        for hh in range(2):
                            h = 2 * j + hh
                            bb = 64 * hh
                            mi = hh * 2 + (j % 2)
                            lsl = Lslots[(j % 2) * 2 + hh]
                            seldeps = lsl.bank.wdeps() + [te_rows]
                            em("tensor", lambda e, h=h, lsl=lsl: e.matmul(
                                lsl.v(0, 256),
                                identf[0:32, h:h + 1].broadcast_to([32, 128]),
                                negp_rows2[:], start=True, stop=False),
                               deps=seldeps, inc=None)
                            t_msk = em("tensor", lambda e, lsl=lsl: e.matmul(
                                lsl.v(128, 256), negmask[:], identbf[:],
                                start=False, stop=True),
                                       deps=[t_nm], inc=sPE)
                            b_rows2.read(t_msk)
                            tGt = em("tensor", lambda e, j=j, c=c, bb=bb, lsl=lsl: e.matmul(
                                lsl.v(256, 384),
                                xbc_tile(16 + j, 3 + c * Q, 3 + (c + 1) * Q)[bb:bb + 64, :],
                                xbc_tile(32 + j, 3 + c * Q, 3 + (c + 1) * Q)[bb:bb + 64, :],
                                start=True, stop=True),
                                     deps=list(tile_dep(32 + j)) + list(tile_dep(16 + j)),
                                     inc=sPE)
                            last_pe_tile[32 + j] = tGt
                            tL = em("scalar", lambda e, h=h, hh=hh, lsl=lsl: e.activation(
                                L_sb[hh][:], lsl.v(128, 256), AF.Exp,
                                bias=negp_tm[:, h:h + 1], scale=-1.0),
                                    deps=[t_msk] + b_Ls[hh].wdeps(), inc=sSC)
                            b_Ls[hh].wrote(tL)
                            b_negptm.read(tL)
                            twb = em("scalar", lambda e, bb=bb, hh=hh, lsl=lsl: e.activation(
                                wb_sb[hh][bb:bb + 64, :], lsl.vp(bb, bb + 64, 0, 128), AF.Exp,
                                scale=-1.0),
                                     deps=[t_msk] + b_wb[hh].wdeps(), inc=sSC)
                            b_wb[hh].wrote(twb)
                            tM = em("vector", lambda e, mi=mi, hh=hh, lsl=lsl: e.tensor_tensor(
                                M_sb4[mi][:], lsl.v(256, 384), L_sb[hh][:], ALU.mult),
                                    deps=[tGt, tL, twb] + b_Ms4[mi].wdeps(), inc=sV)
                            b_Ms4[mi].wrote(tM)
                            b_Ls[hh].read(tM)
                            tC = em("vector", lambda e, j=j, c=c, hh=hh, bb=bb, mi=mi:
                                    e.tensor_tensor(
                                        ct_sb4[mi][bb:bb + 64, :],
                                        xbc_tile(32 + j, 3 + c * Q, 3 + (c + 1) * Q)[bb:bb + 64, :],
                                        wb_sb[hh][bb:bb + 64, :], ALU.mult),
                                    deps=[twb] + b_ct4[mi].wdeps(), inc=sV)
                            b_ct4[mi].wrote(tC)
                            b_wb[hh].read(tC)
                            M_tok[mi] = tM
                            C_tok[mi] = tC
                            lsl.bank.wrote(tGt)
                            lsl.bank.read(tL)
                            lsl.bank.read(twb)
                            lsl.bank.read(tM)
                    skip_state = (not state_only) and (ci == len(chunks) - 1)
                    tsn_toks = []
                    for hh in range(2 if not skip_state else 0):
                        h = 2 * j + hh
                        bb = 64 * hh
                        tS = em("tensor", lambda e, hh=hh, bb=bb, par=par, sl=sl: e.matmul(
                            sl.vp(bb, bb + 64, 256, 320), bt_sb[hh][:],
                            xt_sb[par][:, bb:bb + 64],
                            start=True, stop=True, tile_position=(0, bb)),
                                deps=[bts[hh], tev_x], inc=sPE)
                        bt.wrote(tS)
                        b_xt[par].read(tS)
                        b_bt[hh].read(tS)
                        tSn = em("vector", lambda e, hh=hh, bb=bb, pw=pw, h=h, sl=sl:
                                 e.scalar_tensor_tensor(
                                     S_f[pw][bb:bb + 64, (h // 2) * 64:(h // 2) * 64 + 64],
                                     S_f[1 - pw][bb:bb + 64, (h // 2) * 64:(h // 2) * 64 + 64],
                                     lamb[bb:bb + 64, h:h + 1],
                                     sl.vp(bb, bb + 64, 256, 320), ALU.mult, ALU.add),
                                 deps=[tS], inc=sV)
                        b_lamb.read(tSn)
                        bt.read(tSn)
                        tsn_toks.append(tSn)
                        if not state_only:
                            tSb = em("scalar", lambda e, hh=hh, bb=bb, pw=pw, h=h: e.copy(
                                S_bf[pw][bb:bb + 64, (h // 2) * 64:(h // 2) * 64 + 64],
                                S_f[pw][bb:bb + 64, (h // 2) * 64:(h // 2) * 64 + 64]),
                                     deps=[tSn], inc=sSC)
                            sbf_tok[(pw, h)] = tSb
                    if prev is not None and not state_only and SSD_MODE == "full":
                        emit_pair_back(prev)
                    prev = (c, pw, j, sl, tev_x, tsn_toks, z_dep, state_dep)
                if not state_only and SSD_MODE == "full":
                    emit_pair_back(prev)

        t_upre = load_ubf(u_pre)
        halo_toks = {}
        tap3_pre = {}
        t_a_pre_box = [None]

        def pre_post(oc, t_e):
            if oc == 64:
                t_a_pre_box[0] = softplus_stage(t_e)
                return
            t = oc
            halo_toks[t] = em("vector", lambda e, t=t: e.tensor_copy(
                halo_sb[:, 3 * t:3 * t + 3], xbc_tile(t, 1024, 1027)),
                deps=[t_e], inc=sV)
            conv_tile(t, [t_e, halo_toks[t], hz[t]], tap3_pre)

        evp = proj_stage([64] + list(range(32)), t_upre, z_to_silu=False,
                         extra_evac_deps=lambda oc: [hz.get(oc)], tile_tok=None,
                         post_tile=pre_post)
        t_a_pre = t_a_pre_box[0]

        hdeps0 = bkP[1].wdeps()
        halo_c_mm = None
        for t in range(32, 48):
            buf, bbuf = winsl[t % 2], b_winsl[t % 2]
            t_l = em("gpsimd", lambda e, buf=buf, t=t: e.dma_start(
                buf[:].rearrange("p (kc o) -> p kc o", kc=NKC),
                slab_src(w_inT, t * 128, 128, NKC)),
                     deps=bbuf.wdeps(), inc=sDW)
            bbuf.wrote(t_l)
            o3 = 256 + 3 * (t - 32)
            t_mm = None
            for kc in range(NKC):
                def mmfn(e, buf=buf, kc=kc, o3=o3):
                    return e.matmul(ssdP[1][:, o3:o3 + 3],
                                    buf[:, kc * 128:(kc + 1) * 128],
                                    u_bf[:, kc * 1024 + 1021:kc * 1024 + 1024],
                                    start=(kc == 0), stop=(kc == NKC - 1))
                t_mm = em("tensor", mmfn,
                          deps=(([t_l, t_upre] + (hdeps0 if t == 32 else []))
                                if kc == 0 else ()),
                          inc=(sPE if kc == NKC - 1 else None))
            bbuf.read(t_mm)
            b_ubf.read(t_mm)
            halo_c_mm = t_mm
        bkP[1].wrote(halo_c_mm)
        t_haloc = em("vector", lambda e: e.tensor_copy(halo_sb[:, 96:144],
                                                       ssdP[1][:, 256:304]),
                     deps=[halo_c_mm], inc=sV)
        bkP[1].read(t_haloc)
        last_pe_tile = {}

        def pre_tile_dep(t):
            d = list(tap3_pre.get(t, []))
            return d if d else [evp[t]]

        seed_slots()
        ssd_chunks(list(range(NCHUNK)), 0, True, t_a_pre, pre_tile_dep, None,
                   last_pe_tile, t_s0)
        merge_slots()
        fr = bld.frontier()
        t_gs = em("vector", lambda e: e.tensor_scalar(S_f[1][:], S_f[1][:], gate_c[:],
                                                      None, ALU.mult),
                  deps=fr + [t_gin], inc=sV)
        t_gsb = em("vector", lambda e: e.tensor_copy(S_bf[1][:], S_f[1][:]),
                   deps=[t_gs], inc=sV)

        def trunc_exit():
            frX = bld.frontier()
            lastt = None
            for o in range(16):
                lastt = em("sync", lambda e, o=o: e.dma_start(out_r[:, o, :], x1v(o)),
                           deps=frX, inc=sDS)
            em("sync", lambda e: e.nop(), deps=[lastt], inc=None)

        out_r = out_feat.rearrange("(o p) t -> o p t", p=128).transpose([1, 0, 2])
        if TRUNCATE == "preSSD":
            trunc_exit()
            with nc.Block() as block:
                bld.flush(block)
            return nc

        t_uown = load_ubf(u_own)
        main_tiles = {}
        halow = {}
        tap3_main = {}
        t_a_main_box = [None]

        def main_evac_deps(oc):
            d = []
            if oc < 32:
                d.append(last_pe_tile.get(oc))
                d += tap3_pre.get(oc, [])
                d.append(halo_toks[oc])
            return d

        def main_post(oc, t_e):
            if oc == 64:
                t_a_main_box[0] = softplus_stage(t_e, extra=[last_prol[0]])
                return
            if oc >= 48:
                return
            t = oc
            hdep = tap3_pre.get(t, []) + ([halo_toks[t]] if t < 32 else [t_haloc])
            halow[t] = em("vector", lambda e, t=t: e.tensor_scalar(
                xbc_tile(t, 0, 3), halo_sb[:, 3 * t:3 * t + 3], gate_c[:], None,
                ALU.mult),
                deps=[t_gin] + hdep, inc=sV)
            conv_tile(t, [t_e, halow[t]], tap3_main)

        evm = proj_stage(list(range(32)) + [64] + list(range(32, 48))
                         + list(range(48, 64)), t_uown,
                         z_to_silu=True, extra_evac_deps=main_evac_deps,
                         tile_tok=main_tiles, post_tile=main_post)
        t_a_main = t_a_main_box[0]
        if TRUNCATE == "postA":
            trunc_exit()
            with nc.Block() as block:
                bld.flush(block)
            return nc

        last_pe_tile2 = {}
        seed_slots()
        ssd_chunks(list(range(NCHUNK)), NCHUNK, False, t_a_main,
                   lambda t: list(tap3_main.get(t, [])),
                   lambda zt: evm[48 + zt], last_pe_tile2, t_gsb)
        merge_slots()

        if TRUNCATE == "postB":
            trunc_exit()
            with nc.Block() as block:
                bld.flush(block)
            return nc

        if DEBUG:
            dbg_xbc = nc.dram_tensor("dbg_xbc", [128, 48 * 1027], BF16,
                                     kind="ExternalOutput").ap()
            dbg_a = nc.dram_tensor("dbg_a", [32, 1024], F32, kind="ExternalOutput").ap()
            dbg_silu = nc.dram_tensor("dbg_silu", [128, 16 * 1024], BF16,
                                      kind="ExternalOutput").ap()
            dbg_yg = nc.dram_tensor("dbg_yg", [128, 16 * 1024], BF16,
                                    kind="ExternalOutput").ap()
            dbg_S = nc.dram_tensor("dbg_S", [128, 2048], F32, kind="ExternalOutput").ap()
            frD = bld.frontier()
            em("sync", lambda e: e.dma_start(dbg_xbc, xbc[:]), deps=frD, inc=sDS)
            em("sync", lambda e: e.dma_start(dbg_a, a_fm[:]), deps=frD, inc=sDS)
            em("sync", lambda e: e.dma_start(dbg_silu, silu_z[:]), deps=frD, inc=sDS)
            em("sync", lambda e: e.dma_start(dbg_yg, yg[:]), deps=frD, inc=sDS)
            em("sync", lambda e: e.dma_start(dbg_S[:, 0:1024], S_f[0][:]), deps=frD, inc=sDS)
            em("sync", lambda e: e.dma_start(dbg_S[:, 1024:2048], S_f[1][:]), deps=frD, inc=sDS)

        frB = bld.frontier()
        stat_banks = [ssdP[0], ssdP[1], ssdT[0], ssdT[1]]
        stat_bk = [bkP[0], bkP[1], bkT[0], bkT[1]]
        x1tok = {}
        statmm = {}

        def emit_stats(o, t_xb, t_sq, statmm_out):
            for tt in range(2):
                tmm1 = em("tensor", lambda e, o=o, tt=tt: e.matmul(
                    stat_banks[tt][0:1, 0:512], onescol_bf[:],
                    lnxb[o % 2][:, tt * 512:(tt + 1) * 512],
                    start=(o == 0), stop=(o == 15)),
                          deps=[t_xb, t_ocb] + (stat_bk[tt].wdeps() if o == 0 else []),
                          inc=sPE)
                tmm2 = em("tensor", lambda e, o=o, tt=tt: e.matmul(
                    stat_banks[2 + tt][0:1, 0:512], onescol_bf[:],
                    lnsq[o % 2][:, tt * 512:(tt + 1) * 512],
                    start=(o == 0), stop=(o == 15)),
                          deps=[t_sq] + (stat_bk[2 + tt].wdeps() if o == 0 else []),
                          inc=sPE)
                if o == 15:
                    statmm_out[tt] = (tmm1, tmm2)
                b_lnxb[o % 2].read(tmm1)
                b_lnsq[o % 2].read(tmm2)
                if o == 15:
                    stat_bk[tt].wrote(tmm1)
                    stat_bk[2 + tt].wrote(tmm2)

        pend_stats = [None]
        for o in range(16):
            buf, bbuf = woutsl[o % 2], b_wout[o % 2]
            t_l = em("gpsimd", lambda e, buf=buf, o=o: e.dma_start(
                buf[:].rearrange("p (kc o) -> p kc o", kc=NKC),
                slab_src(w_outT, o * 128, 128, NKC)),
                     deps=bbuf.wdeps() + (frB if o < 2 else []), inc=sDW)
            bbuf.wrote(t_l)
            t_u = em("sync", lambda e, o=o: e.dma_start(
                ures[o % 2][:], u_own[o * 128:(o + 1) * 128, :]),
                     deps=b_ures[o % 2].wdeps() + (frB if o < 2 else []), inc=sDS)
            b_ures[o % 2].wrote(t_u)
            pr, bk = pair_banks[o % 2]
            first_deps = [t_l] + bk.wdeps() + (frB if o < 2 else [])
            t_mm = None
            for kc in range(NKC):
                for tt in range(2):
                    last = (kc == NKC - 1 and tt == 1)

                    def mmfn(e, buf=buf, kc=kc, tt=tt, pr=pr):
                        return e.matmul(pr[:, tt * 512:(tt + 1) * 512],
                                        buf[:, kc * 128:(kc + 1) * 128],
                                        yg[:, kc * 1024 + tt * 512:kc * 1024 + (tt + 1) * 512],
                                        start=(kc == 0), stop=(kc == NKC - 1))
                    t_mm = em("tensor", mmfn, deps=(first_deps if (kc == 0 and tt == 0) else ()),
                              inc=(sPE if last else None))
            if pend_stats[0] is not None:
                emit_stats(*pend_stats[0])
                pend_stats[0] = None
            bk.wrote(t_mm)
            bbuf.read(t_mm)
            t_e = em("vector", lambda e, o=o, pr=pr: e.tensor_tensor(
                x1v(o), ures[o % 2][:], pr[:], ALU.add),
                     deps=[t_mm, t_u] + (frB if o < 2 else []), inc=sV)
            b_ures[o % 2].read(t_e)
            bk.read(t_e)
            x1tok[o] = t_e
            t_xb = em("vector", lambda e, o=o: e.tensor_copy(lnxb[o % 2][:], x1v(o)),
                      deps=[t_e] + b_lnxb[o % 2].wdeps(), inc=sV)
            b_lnxb[o % 2].wrote(t_xb)
            t_sq = em("scalar", lambda e, o=o: e.activation(lnsq[o % 2][:], x1v(o), AF.Square),
                      deps=[t_e] + b_lnsq[o % 2].wdeps(), inc=sSC)
            b_lnsq[o % 2].wrote(t_sq)
            pend_stats[0] = (o, t_xb, t_sq, statmm)
        emit_stats(*pend_stats[0])
        pend_stats[0] = None

        def ln_finalize_apply(x1dep, statmm, out_bf):
            toks = {}
            for tt in range(2):
                tsum, tsq = statmm[tt]
                tm = em("vector", lambda e, tt=tt: e.tensor_scalar(
                    ln_m[:], stat_banks[tt][0:1, 0:512], 1.0 / DM, None, ALU.mult),
                        deps=[tsum] + b_lnm.wdeps(), inc=sV)
                b_lnm.wrote(tm)
                stat_bk[tt].read(tm)
                tmsq = em("vector", lambda e, tt=tt: e.tensor_scalar(
                    ln_msq[:], stat_banks[2 + tt][0:1, 0:512], 1.0 / DM, None, ALU.mult),
                          deps=[tsq] + b_lnmsq.wdeps(), inc=sV)
                b_lnmsq.wrote(tmsq)
                stat_bk[2 + tt].read(tmsq)
                tm2 = em("scalar", lambda e: e.activation(ln_m2[:], ln_m[:], AF.Square),
                         deps=[tm] + b_lnm2.wdeps(), inc=sSC)
                b_lnm2.wrote(tm2)
                b_lnm.read(tm2)
                tv = em("vector", lambda e: e.tensor_tensor(ln_var[:], ln_msq[:], ln_m2[:],
                                                            ALU.subtract),
                        deps=[tmsq, tm2] + b_lnvar.wdeps(), inc=sV)
                b_lnvar.wrote(tv)
                b_lnmsq.read(tv)
                b_lnm2.read(tv)
                tl = em("scalar", lambda e: e.activation(ln_rstd[:], ln_var[:], AF.Ln,
                                                         bias=eps_c[:]),
                        deps=[tv, t_eps] + b_lnrstd.wdeps(), inc=sSC)
                tr = em("scalar", lambda e: e.activation(ln_rstd[:], ln_rstd[:], AF.Exp,
                                                         scale=-0.5),
                        deps=[tl], inc=sSC)
                b_lnrstd.wrote(tr)
                b_lnvar.read(tl)
                dA2 = bkA.wdeps()
                tbm = em("tensor", lambda e: e.matmul(pairA[:, 0:512], onesrow[:], ln_m[:],
                                                      start=True, stop=True),
                         deps=[tm, t_or] + dA2, inc=sPE)
                b_lnm.read(tbm)
                tbr = em("tensor", lambda e: e.matmul(pairA[:, 512:1024], onesrow[:], ln_rstd[:],
                                                      start=True, stop=True),
                         deps=[tr], inc=sPE)
                b_lnrstd.read(tbr)
                bkA.wrote(tbr)
                tmb = em("scalar", lambda e: e.copy(mb_sb[:], pairA[:, 0:512]),
                         deps=[tbm] + b_mb.wdeps(), inc=sSC)
                b_mb.wrote(tmb)
                bkA.read(tmb)
                for o in range(16):
                    t1 = em("vector", lambda e, o=o, tt=tt: e.tensor_tensor(
                        x1v(o)[:, tt * 512:(tt + 1) * 512],
                        x1v(o)[:, tt * 512:(tt + 1) * 512], mb_sb[:], ALU.subtract),
                            deps=[tmb, x1dep(o, tt)], inc=sV)
                    b_mb.read(t1)
                    t2 = em("vector", lambda e, o=o, tt=tt: e.tensor_tensor(
                        x1v(o)[:, tt * 512:(tt + 1) * 512],
                        x1v(o)[:, tt * 512:(tt + 1) * 512], pairA[:, 512:1024], ALU.mult),
                            deps=[t1, tbr], inc=sV)
                    bkA.read(t2)
                    tok = t2
                    if out_bf is not None:
                        tok = em("vector", lambda e, o=o, tt=tt: e.tensor_copy(
                            out_bf[:, o * 1024 + tt * 512:o * 1024 + (tt + 1) * 512],
                            x1v(o)[:, tt * 512:(tt + 1) * 512]),
                                 deps=[t2], inc=sV)
                    toks[(o, tt)] = tok
            return toks

        ln1 = ln_finalize_apply(lambda o, tt: x1tok[o], statmm, x1_bf)

        if DEBUG:
            dbg_x1 = nc.dram_tensor("dbg_x1", [128, 16 * 1024], F32,
                                    kind="ExternalOutput").ap()
            for o in range(16):
                em("sync", lambda e, o=o: e.dma_start(dbg_x1[:, o * 1024:(o + 1) * 1024],
                                                      x1v(o)),
                   deps=[ln1[(o, 0)], ln1[(o, 1)]], inc=sDS)

        if TRUNCATE == "postC":
            trunc_exit()
            with nc.Block() as block:
                bld.flush(block)
            return nc

        frC = bld.frontier()
        down_done = {}
        gate_banks = [(pairA, pairB, bkA, bkB),
                      (ssdP[0], ssdT[0], bkP[0], bkT[0])]
        lastln = [ln1[(15, 0)], ln1[(15, 1)]]
        for hb in range(2):
            gtoks = {}
            for m_ in range(32):
                m = hb * 32 + m_
                ex = frC if (hb == 0 and m_ < 2) else []
                bg, bu = b_wg[m_ % 2], b_wu[m_ % 2]
                t_lg = em("gpsimd", lambda e, m=m, m_=m_: e.dma_start(
                    wgsl[m_ % 2][:].rearrange("p (kc o) -> p kc o", kc=NKC),
                    slab_src(wgT, m * 128, 128, NKC)),
                          deps=bg.wdeps() + ex, inc=sDW)
                bg.wrote(t_lg)
                t_lu = em("gpsimd", lambda e, m=m, m_=m_: e.dma_start(
                    wusl[m_ % 2][:].rearrange("p (kc o) -> p kc o", kc=NKC),
                    slab_src(wuT, m * 128, 128, NKC)),
                          deps=bu.wdeps(), inc=sDW)
                bu.wrote(t_lu)
                if m_ % 2 == 0:
                    prg, pru, bkg, bku = pairA, pairB, bkA, bkB
                    gv = lambda tt: prg[:, tt * 512:(tt + 1) * 512]
                    uv = lambda tt: pru[:, tt * 512:(tt + 1) * 512]
                else:
                    gv = lambda tt: [ssdP[0], ssdP[1]][tt][:, 0:512]
                    uv = lambda tt: [ssdT[0], ssdT[1]][tt][:, 0:512]
                    bkg, bku = None, None
                if m_ % 2 == 0:
                    bdeps = bkA.wdeps() + bkB.wdeps()
                else:
                    bdeps = (bkP[0].wdeps() + bkP[1].wdeps()
                             + bkT[0].wdeps() + bkT[1].wdeps())
                first_deps = [t_lg, t_lu] + lastln + bdeps + ex
                t_mm = None
                for kc in range(NKC):
                    for tt in range(2):
                        for gu in range(2):
                            last = (kc == NKC - 1 and tt == 1 and gu == 1)

                            def mmfn(e, m_=m_, kc=kc, tt=tt, gu=gu, gv=gv, uv=uv):
                                w = (wgsl if gu == 0 else wusl)[m_ % 2]
                                pv = (gv if gu == 0 else uv)(tt)
                                return e.matmul(
                                    pv, w[:, kc * 128:(kc + 1) * 128],
                                    x1_bf[:, kc * 1024 + tt * 512:kc * 1024 + (tt + 1) * 512],
                                    start=(kc == 0), stop=(kc == NKC - 1))
                            t_mm = em("tensor", mmfn,
                                      deps=(first_deps if (kc == 0 and tt == 0 and gu == 0) else ()),
                                      inc=(sPE if last else None))
                bg.read(t_mm)
                bu.read(t_mm)
                gts = []
                for tt in range(2):
                    si = tt * 2 + m_ % 2
                    t_sg = em("scalar", lambda e, tt=tt, si=si, gv=gv: e.activation(
                        silu4[si][:], gv(tt), AF.Silu),
                              deps=[t_mm] + b_sil4[si].wdeps() + ex, inc=sSC)
                    b_sil4[si].wrote(t_sg)
                    t_gt = em("vector", lambda e, m_=m_, tt=tt, si=si, uv=uv: e.tensor_tensor(
                        gated[:, m_ * 1024 + tt * 512:m_ * 1024 + (tt + 1) * 512],
                        silu4[si][:], uv(tt), ALU.mult),
                              deps=[t_sg] + ex, inc=sV)
                    b_sil4[si].read(t_gt)
                    gts.append(t_gt)
                if m_ % 2 == 0:
                    bkA.wrote(t_mm); bkB.wrote(t_mm)
                    bkA.read(gts[0]); bkA.read(gts[1])
                    bkB.read(gts[0]); bkB.read(gts[1])
                else:
                    for bx, gt in ((bkP[0], gts[0]), (bkP[1], gts[1])):
                        bx.wrote(t_mm); bx.read(gt)
                    for bx in (bkT[0], bkT[1]):
                        bx.wrote(t_mm); bx.read(gts[0]); bx.read(gts[1])
                gtoks[m_] = gts
            glast = gtoks[31][1]
            for o in range(16):
                bw = b_wd[o % 2]
                t_ld = em("gpsimd", lambda e, o=o, hb=hb: e.dma_start(
                    wdsl[o % 2][:].rearrange("p (kc o) -> p kc o", kc=32),
                    wdT[hb * 4096:(hb + 1) * 4096, o * 128:(o + 1) * 128]
                    .rearrange("(kc p) o -> kc p o", p=128).transpose([1, 0, 2])),
                          deps=bw.wdeps() + (frC if (hb == 0 and o == 0) else []), inc=sDW)
                bw.wrote(t_ld)
                pr, bk = (pairA, bkA) if o % 2 == 0 else (pairB, bkB)
                t_mm = None
                for kc in range(32):
                    for tt in range(2):
                        last = (kc == 31 and tt == 1)
                        if kc == 0 and tt == 0:
                            deps0 = bk.wdeps() + [t_ld, glast]
                        else:
                            deps0 = ()

                        def mmfn(e, o=o, kc=kc, tt=tt, pr=pr):
                            return e.matmul(
                                pr[:, tt * 512:(tt + 1) * 512],
                                wdsl[o % 2][:, kc * 128:(kc + 1) * 128],
                                gated[:, kc * 1024 + tt * 512:kc * 1024 + (tt + 1) * 512],
                                start=(kc == 0), stop=(kc == 31))
                        t_mm = em("tensor", mmfn, deps=deps0, inc=(sPE if last else None))
                bk.wrote(t_mm)
                bw.read(t_mm)
                for tt in range(2):
                    t_add = em("vector", lambda e, o=o, tt=tt, pr=pr: e.tensor_tensor(
                        x1v(o)[:, tt * 512:(tt + 1) * 512],
                        x1v(o)[:, tt * 512:(tt + 1) * 512], pr[:, tt * 512:(tt + 1) * 512],
                        ALU.add),
                               deps=[t_mm, ln1[(o, tt)]]
                               + ([down_done[(o, tt)]] if hb == 1 else []), inc=sV)
                    bk.read(t_add)
                    down_done[(o, tt)] = t_add

        frD2 = bld.frontier()
        statmm2 = {}
        for o in range(16):
            dd = [down_done[(o, 0)], down_done[(o, 1)]] + (frD2 if o == 0 else [])
            t_xb = em("vector", lambda e, o=o: e.tensor_copy(lnxb[o % 2][:], x1v(o)),
                      deps=dd + b_lnxb[o % 2].wdeps(), inc=sV)
            b_lnxb[o % 2].wrote(t_xb)
            t_sq = em("scalar", lambda e, o=o: e.activation(lnsq[o % 2][:], x1v(o), AF.Square),
                      deps=dd + b_lnsq[o % 2].wdeps(), inc=sSC)
            b_lnsq[o % 2].wrote(t_sq)
            if pend_stats[0] is not None:
                emit_stats(*pend_stats[0])
                pend_stats[0] = None
            pend_stats[0] = (o, t_xb, t_sq, statmm2)
        emit_stats(*pend_stats[0])
        pend_stats[0] = None
        if DEBUG:
            dbg_st = nc.dram_tensor("dbg_st", [4, 512], F32, kind="ExternalOutput").ap()
            stt = []
            for i in range(4):
                tt_ = i % 2
                src = stat_banks[i // 2 * 2 + tt_] if False else stat_banks[[0, 1, 2, 3][i]]
                tcp = em("vector", lambda e, i=i: e.tensor_copy(
                    mb_sb[0:1, 0:512], stat_banks[i][0:1, 0:512]),
                         deps=[statmm2[0][0], statmm2[0][1], statmm2[1][0], statmm2[1][1]]
                         + b_mb.wdeps(), inc=sV)
                b_mb.wrote(tcp)
                td = em("sync", lambda e, i=i: e.dma_start(dbg_st[i:i + 1, :], mb_sb[0:1, 0:512]),
                        deps=[tcp], inc=sDS)
                b_mb.read(td)
                stt.append(td)
        ln2 = ln_finalize_apply(lambda o, tt: down_done[(o, tt)], statmm2, None)

        last = None
        for o in range(16):
            last = em("sync", lambda e, o=o: e.dma_start(out_r[:, o, :], x1v(o)),
                      deps=[ln2[(o, 0)], ln2[(o, 1)]], inc=sDS)
        em("sync", lambda e: e.nop(), deps=[last], inc=None)

        with nc.Block() as block:
            bld.flush(block)

    return nc


_NC_CACHE = None


def kernel(L0_attn_u, L0_attn_W_in, L0_attn_conv_w, L0_attn_conv_b, L0_attn_z_bias,
           L0_attn_D, L0_attn_W_out, L0_ffn_w_gate, L0_ffn_w_up, L0_ffn_w_down):
    global _NC_CACHE
    u = np.asarray(L0_attn_u, np.float32)
    w_inT = np.ascontiguousarray(np.asarray(L0_attn_W_in, np.float32).T)
    w_outT = np.ascontiguousarray(np.asarray(L0_attn_W_out, np.float32).T)
    wgT = np.ascontiguousarray(np.asarray(L0_ffn_w_gate, np.float32).T)
    wuT = np.ascontiguousarray(np.asarray(L0_ffn_w_up, np.float32).T)
    wdT = np.ascontiguousarray(np.asarray(L0_ffn_w_down, np.float32).T)
    convw = np.ascontiguousarray(
        np.asarray(L0_attn_conv_w, np.float32).reshape(48, 128, 4).transpose(1, 0, 2)
        .reshape(128, 192))
    convb = np.ascontiguousarray(np.asarray(L0_attn_conv_b, np.float32).reshape(48, 128).T)
    zb = np.ascontiguousarray(np.asarray(L0_attn_z_bias, np.float32).reshape(16, 128).T)
    drow = np.ascontiguousarray(np.asarray(L0_attn_D, np.float32).reshape(1, NH))

    identf = np.eye(128, dtype=np.float32)
    identbf = np.eye(128).astype(bf16)
    triu = np.triu(np.ones((128, 128), np.float32))
    ii = np.arange(128)
    negmaskT = np.where(ii[:, None] < ii[None, :], BIG, 0.0).astype(np.float32).astype(bf16)
    onesrow = np.ones((1, 128), np.float32)

    u_feat = np.ascontiguousarray(u.transpose(0, 2, 1))

    if _NC_CACHE is None:
        _NC_CACHE = build_nc()
    nc = _NC_CACHE

    in_maps = []
    for c in range(8):
        b, half = c // 2, c % 2
        own = np.ascontiguousarray(u_feat[b][:, half * 1024:(half + 1) * 1024])
        if half == 1:
            pre = np.ascontiguousarray(u_feat[b][:, 0:1024])
            gate = np.ones((128, 1), np.float32)
        else:
            pre = np.zeros((DM, TOWN), np.float32)
            gate = np.zeros((128, 1), np.float32)
        in_maps.append({
            "u_own": own, "u_pre": pre, "gate": gate,
            "w_inT": w_inT, "w_outT": w_outT, "wgT": wgT, "wuT": wuT, "wdT": wdT,
            "convw": convw, "convb": convb, "zb": zb, "drow": drow,
            "identf": identf, "identbf": identbf, "triu": triu,
            "negmaskT": negmaskT, "onesrow": onesrow,
        })

    res = run_bass_kernel_spmd(nc, in_maps, core_ids=list(range(8)))
    global _LAST_RES
    _LAST_RES = res
    out = np.empty((4, 2048, 2048), np.float32)
    for c in range(8):
        b, half = c // 2, c % 2
        out[b, half * 1024:(half + 1) * 1024, :] = res.results[c]["out_feat"].T
    return out
